# version 09
# baseline (speedup 1.0000x reference)
import numpy as np
import ml_dtypes
from contextlib import ExitStack

import concourse.bass as bass
import concourse.mybir as mybir
from concourse.bass_utils import run_bass_kernel_spmd
from concourse.alu_op_type import AluOpType as OP

F32 = mybir.dt.float32
BF16 = mybir.dt.bfloat16
AF = mybir.ActivationFunctionType

TEN, VEC, SCA, GPS, SYN = "tensor", "vector", "scalar", "gpsimd", "sync"
ENGS = (TEN, VEC, SCA, GPS, SYN)

HID = 2048
NH = 16
Q = 256
EPS = 1e-5
T = 2048
TG = 1024
HBW = TG + 3
PREW = T + 3

DBG = False


class Ctr:
    def __init__(self, sem, owner=None):
        self.sem = sem
        self.val = 0
        self.owner = owner

    def bump(self, n):
        self.val += n
        return (self, n)

    def at(self, val=None):
        return (self, self.val if val is None else val)


class Sched:
    def __init__(self):
        self.q = {e: [] for e in ENGS}
        self.waited = {e: {} for e in ENGS}

    def op(self, eng, method, *args, waits=(), incs=(), **kwargs):
        ws = []
        for w in waits:
            if w is None:
                continue
            c, val = w
            if c.owner == eng or val <= 0:
                continue
            key = c.sem.num
            if self.waited[eng].get(key, -1) >= val:
                continue
            self.waited[eng][key] = val
            ws.append((c.sem, val))
        ics = []
        for c, n in incs:
            ics.append((c.sem, n))
        self.q[eng].append((method, args, kwargs, ws, ics))

    def barrier(self, bctr):
        target = bctr.val + 5
        for e in ENGS:
            self.q[e].append(("sem_inc", (bctr.sem, 1), {}, [], []))
        bctr.val = target
        for e in ENGS:
            self.q[e].append(("wait_ge", (bctr.sem, target), {}, [], []))
            self.waited[e][bctr.sem.num] = target

    def replay(self, eng_name, eng):
        for method, args, kwargs, ws, ics in self.q[eng_name]:
            for sem, val in ws:
                eng.wait_ge(sem, val)
            inst = getattr(eng, method)(*args, **kwargs)
            for sem, n in ics:
                inst.then_inc(sem, n, skip_validation=True)


class Bank:
    def __init__(self, t):
        self.t = t
        self.pending = []

    def take(self):
        ws = self.pending
        self.pending = []
        return ws

    def release(self, ctr):
        self.pending.append(ctr.at())


def build_nc():
    nc = bass.Bass("TRN2", target_bir_lowering=False)

    x_fm = nc.dram_tensor("x_fm", [HID, T], F32, kind="ExternalInput")
    xpre_fm = nc.dram_tensor("xpre_fm", [HID, T], F32, kind="ExternalInput")
    w_in = nc.dram_tensor("w_in", [HID, 5136], F32, kind="ExternalInput")
    w_out = nc.dram_tensor("w_out", [HID, HID], F32, kind="ExternalInput")
    w_up_t = nc.dram_tensor("w_up_t", [HID, 8192], F32, kind="ExternalInput")
    w_dn_t = nc.dram_tensor("w_dn_t", [8192, HID], F32, kind="ExternalInput")
    conv_w = nc.dram_tensor("conv_w", [128, 96], F32, kind="ExternalInput")
    conv_b = nc.dram_tensor("conv_b", [128, 24], F32, kind="ExternalInput")
    alog_d = nc.dram_tensor("alog_d", [NH, 1], F32, kind="ExternalInput")
    dtb_d = nc.dram_tensor("dtb_d", [NH, 1], F32, kind="ExternalInput")
    drep_d = nc.dram_tensor("drep_d", [128, NH], F32, kind="ExternalInput")
    nwst_d = nc.dram_tensor("nwst_d", [128, 16], F32, kind="ExternalInput")
    upsc_d = nc.dram_tensor("upsc_d", [128, 64], F32, kind="ExternalInput")
    dnsc_d = nc.dram_tensor("dnsc_d", [128, 16], F32, kind="ExternalInput")
    outsc_d = nc.dram_tensor("outsc_d", [128, 1], F32, kind="ExternalInput")
    s0sel_d = nc.dram_tensor("s0sel_d", [128, 1], F32, kind="ExternalInput")

    out_fm = nc.dram_tensor("out_fm", [HID, T], F32, kind="ExternalOutput")
    skind = "ExternalOutput" if DBG else "Internal"
    x1_scr = nc.dram_tensor("x1_scr", [HID, T], BF16, kind=skind)
    hid_scr = nc.dram_tensor("hid_scr", [8192, T], BF16, kind=skind)
    wdn_scr = nc.dram_tensor("wdn_scr", [HID, 8192], BF16, kind="Internal")
    if DBG:
        dbg_hbc = nc.dram_tensor("dbg_hbc", [128, 24 * HBW], BF16,
                                 kind="ExternalOutput")
        dbg_dt = nc.dram_tensor("dbg_dt", [16, T], F32, kind="ExternalOutput")
        dbg_s = nc.dram_tensor("dbg_s", [128, NH * 128], F32,
                               kind="ExternalOutput")
        dbg_g = nc.dram_tensor("dbg_g", [128, NH * Q], BF16,
                               kind="ExternalOutput")
        dbg_rdt = nc.dram_tensor("dbg_rdt", [16, T], F32,
                                 kind="ExternalOutput")
        dbg_dtt = nc.dram_tensor("dbg_dtt", [128, 128], F32,
                                 kind="ExternalOutput")
        dbg_pre = nc.dram_tensor("dbg_pre", [128, 8 * PREW], BF16,
                                 kind="ExternalOutput")
        dbg_s2 = nc.dram_tensor("dbg_s2", [128, NH * 128], F32,
                                kind="ExternalOutput")
        dbg_xh = nc.dram_tensor("dbg_xh", [128, 1024], BF16,
                                kind="ExternalOutput")
        dbg_bt = nc.dram_tensor("dbg_bt", [128, 1024], BF16,
                                kind="ExternalOutput")
        dbg_dld = nc.dram_tensor("dbg_dld", [16, 1024], F32,
                                 kind="ExternalOutput")

    ident_h = nc.inline_tensor(np.eye(128, dtype=np.float32), "ident")
    ones1_h = nc.inline_tensor(np.ones((1, 128), np.float32), "ones1")
    onesc_h = nc.inline_tensor(
        np.ones((128, 1), np.float32).astype(ml_dtypes.bfloat16), "onescol")
    oh_np = np.zeros((128, 16 * 16), np.float32)
    for h in range(16):
        oh_np[:, h * 16 + h] = 1.0
    oh_h = nc.inline_tensor(oh_np.astype(ml_dtypes.bfloat16), "oh16")
    sel_np = np.zeros((128, 16 * 128), np.float32)
    for h in range(16):
        sel_np[h, h * 128:(h + 1) * 128] = 1.0
    sel_h = nc.inline_tensor(sel_np.astype(ml_dtypes.bfloat16), "selbf")
    mk = np.zeros((128, 512), np.float32)
    for jc in range(2):
        for p_ in range(128):
            for i in range(256):
                if i < jc * 128 + p_:
                    mk[p_, jc * 256 + i] = -1e30
    mask_h = nc.inline_tensor(mk.astype(ml_dtypes.bfloat16), "maskadd")

    es = ExitStack()
    with es:
        sbuf = lambda n, sh, dt: es.enter_context(nc.sbuf_tensor(n, sh, dt))
        sch = Sched()
        ctr = {}
        for name, eng in [
            ("pe", TEN), ("ve", VEC), ("ac", SCA),
            ("c_const", GPS), ("c_xg", GPS), ("c_win0", GPS), ("c_win1", GPS),
            ("c_hw0", GPS), ("c_hw1", GPS), ("c_hw2", GPS), ("c_hw3", GPS),
            ("c_wup0", SYN), ("c_wup1", SYN), ("c_wdn0", SYN), ("c_wdn1", SYN),
            ("c_tr0", SYN), ("c_tr1", SYN), ("c_x1w0", SYN), ("c_x1w1", SYN),
            ("c_ldr", SYN), ("c_ow0", SYN), ("c_ow1", SYN),
            ("c_ldr2", GPS), ("bar", None),
        ]:
            sem = es.enter_context(nc.semaphore(name))
            ctr[name] = Ctr(sem, eng)
        pe, ve, ac = ctr["pe"], ctr["ve"], ctr["ac"]

        pb = [Bank(es.enter_context(nc.psum_tensor(f"pb{i}", [128, 512], F32)))
              for i in range(8)]

        identb = sbuf("identb", [128, 128], F32)
        identbf = sbuf("identbf", [128, 128], BF16)
        ones1b = sbuf("ones1b", [1, 128], F32)
        onescb = sbuf("onescb", [128, 1], BF16)
        ohb = sbuf("ohb", [128, 256], BF16)
        selb = sbuf("selb", [128, 2048], BF16)
        maskb = sbuf("maskb", [128, 512], BF16)
        convwb = sbuf("convwb", [128, 96], F32)
        convbb = sbuf("convbb", [128, 24], F32)
        drepb = sbuf("drepb", [128, 16], F32)
        nwstb = sbuf("nwstb", [128, 16], F32)
        upscb = sbuf("upscb", [128, 64], F32)
        dnscb = sbuf("dnscb", [128, 16], F32)
        outscb = sbuf("outscb", [128, 1], F32)
        s0selb = sbuf("s0selb", [128, 1], F32)
        alogb = sbuf("alogb", [16, 1], F32)
        dtbb = sbuf("dtbb", [16, 1], F32)
        avecb = sbuf("avecb", [16, 1], F32)
        dsob = sbuf("dsob", [128, 16], F32)
        dtfull = sbuf("dtfull", [16, T], F32)
        dach = sbuf("dach", [16, Q], F32)
        cumch = sbuf("cumch", [16, Q], F32)
        ecumch = sbuf("ecumch", [16, Q], F32)
        dldtch = sbuf("dldtch", [16, Q], F32)
        cdch = sbuf("cdch", [16, Q], F32)
        lastc = sbuf("lastc", [16, 1], F32)
        dte1 = sbuf("dte1", [16, 512], F32)
        dte2 = sbuf("dte2", [16, 512], F32)
        crowb = sbuf("crowb", [1, 512], BF16)
        ones16 = sbuf("ones16", [16, Q], F32)
        rtmp = sbuf("rtmp", [16, Q], F32)
        cumsp = sbuf("cumsp", [128, 3 * Q], BF16)
        ecusp = sbuf("ecusp", [128, 2 * Q], BF16)
        rssp = sbuf("rssp", [128, 2 * Q], BF16)
        gnrs = sbuf("gnrs", [16, Q], F32)
        dtT = sbuf("dtT", [128, 128], F32)
        S = sbuf("S", [128, NH * 128], F32)
        halo24 = sbuf("halo24", [128, 24 * 3], BF16)
        xpreh = sbuf("xpreh", [128, 16 * 3], BF16)
        sqt = sbuf("sqt", [128, 1024], BF16)
        epsb = sbuf("epsb", [128, 1], F32)
        scl128b = sbuf("scl128b", [128, 1], F32)

        cc = ctr["c_const"]
        for dst, src in [(identb, ident_h), (ones1b, ones1_h),
                         (onescb, onesc_h), (ohb, oh_h), (selb, sel_h),
                         (maskb, mask_h)]:
            sch.op(GPS, "dma_start", dst[:], src[:, :], incs=[cc.bump(16)])
        sch.op(GPS, "dma_start", convwb[:], conv_w[:, :],
               incs=[cc.bump(16)])
        sch.op(GPS, "dma_start", convbb[:], conv_b[:, :], incs=[cc.bump(16)])
        for dst, src in [(drepb, drep_d), (nwstb, nwst_d), (upscb, upsc_d),
                         (dnscb, dnsc_d), (outscb, outsc_d),
                         (s0selb, s0sel_d), (alogb, alog_d), (dtbb, dtb_d)]:
            sch.op(GPS, "dma_start", dst[:], src[:, :], incs=[cc.bump(16)])
        sch.op(SCA, "activation", avecb[:], alogb[:], AF.Exp,
               waits=[cc.at()], incs=[ac.bump(1)])
        sch.op(VEC, "tensor_scalar", avecb[:], avecb[:], -1.0, None, OP.mult,
               waits=[ac.at()], incs=[ve.bump(1)])
        sch.op(VEC, "tensor_scalar", dsob[:], dnscb[:], outscb[:, 0:1], None,
               OP.mult, waits=[cc.at()], incs=[ve.bump(1)])
        sch.op(VEC, "tensor_copy", identbf[:], identb[:], waits=[cc.at()],
               incs=[ve.bump(1)])
        sch.op(VEC, "memset", ones16[:], 1.0, incs=[ve.bump(1)])
        sch.op(VEC, "memset", epsb[:], EPS, incs=[ve.bump(1)])
        sch.op(VEC, "memset", scl128b[:], 1.0 / 128.0, incs=[ve.bump(1)])
        sch.op(VEC, "memset", S[:], 0.0, incs=[ve.bump(1)])
        sch.op(VEC, "memset", cumsp[:], 0.0, incs=[ve.bump(1)])
        sch.op(VEC, "memset", ecusp[:], 0.0, incs=[ve.bump(1)])
        sch.op(VEC, "memset", rssp[:], 0.0, incs=[ve.bump(1)])

        def split2(dst_sp, src, n=Q):
            sch.op(VEC, "tensor_copy", dst_sp[0:16, 0:n], src,
                   waits=[(pe, pe.val)], incs=[ve.bump(1)])
            sch.op(VEC, "tensor_tensor", rtmp[:, 0:n], src,
                   dst_sp[0:16, 0:n], OP.subtract, incs=[ve.bump(1)])
            sch.op(VEC, "tensor_copy", dst_sp[0:16, n:2 * n], rtmp[:, 0:n],
                   incs=[ve.bump(1)])

        def split3(dst_sp, src, n=Q):
            sch.op(VEC, "tensor_copy", dst_sp[0:16, 0:n], src,
                   waits=[(pe, pe.val)], incs=[ve.bump(1)])
            sch.op(VEC, "tensor_tensor", rtmp[:, 0:n], src,
                   dst_sp[0:16, 0:n], OP.subtract, incs=[ve.bump(1)])
            sch.op(VEC, "tensor_copy", dst_sp[0:16, n:2 * n], rtmp[:, 0:n],
                   incs=[ve.bump(1)])
            sch.op(VEC, "tensor_tensor", rtmp[:, 0:n], rtmp[:, 0:n],
                   dst_sp[0:16, n:2 * n], OP.subtract, incs=[ve.bump(1)])
            sch.op(VEC, "tensor_copy", dst_sp[0:16, 2 * n:3 * n],
                   rtmp[:, 0:n], incs=[ve.bump(1)])

        def dt_block_evac(bank, tslice, e1, e2):
            sch.op(SCA, "activation", e1, bank.t[0:16, 0:512], AF.Exp,
                   bias=dtbb[:, 0:1], waits=[pe.at()], incs=[ac.bump(1)])
            bank.release(ac)
            sch.op(VEC, "tensor_scalar", e2, e1, 1.0, None, OP.add,
                   waits=[ac.at()], incs=[ve.bump(1)])
            sch.op(SCA, "activation", dtfull[:, tslice], e2, AF.Ln,
                   waits=[ve.at()], incs=[ac.bump(1)])

        def dt_chunk(tok0, need_scores):
            dsl = dtfull[:, tok0:tok0 + Q]
            sch.op(VEC, "tensor_scalar", dach[:], dsl, avecb[:, 0:1], None,
                   OP.mult,
                   waits=[ac.at(), (pe, pe.val)]
                   + ([ctr["c_ow0"].at()] if DBG else []),
                   incs=[ve.bump(1)])
            sch.op(VEC, "tensor_tensor_scan", cumch[:], ones16[:], dach[:],
                   0.0, OP.mult, OP.add, incs=[ve.bump(1)])
            sch.op(SCA, "activation", ecumch[:], cumch[:], AF.Exp,
                   waits=[ve.at()], incs=[ac.bump(1)])
            sch.op(SCA, "copy", lastc[:], cumch[:, Q - 1:Q],
                   incs=[ac.bump(1)])
            sch.op(VEC, "tensor_scalar", dldtch[:], cumch[:],
                   lastc[:, 0:1], -1.0, OP.subtract, OP.mult,
                   waits=[ac.at()], incs=[ve.bump(1)])
            sch.op(SCA, "activation", dldtch[:], dldtch[:], AF.Exp,
                   waits=[ve.at()], incs=[ac.bump(1)])
            sch.op(VEC, "tensor_tensor", dldtch[:], dldtch[:], dsl, OP.mult,
                   waits=[ac.at()], incs=[ve.bump(1)])
            sch.op(VEC, "tensor_scalar", cdch[:], cumch[:], 0.0,
                   ecumch[:, Q - 1:Q], OP.mult, OP.add, waits=[ac.at()],
                   incs=[ve.bump(1)])
            if need_scores:
                split3(cumsp, cumch[:])
                split2(ecusp, ecumch[:])
            bk = pb[6]
            w0 = bk.take() + [ve.at(), ac.at(), cc.at(), (pe, pe.val)]
            first = True
            for q, src in enumerate([cumch, dldtch, None, cdch]):
                for jc in range(2):
                    inp = (dtfull[:, tok0 + jc * 128:tok0 + (jc + 1) * 128]
                           if src is None
                           else src[:, jc * 128:(jc + 1) * 128])
                    sch.op(TEN, "transpose",
                           bk.t[:, jc * 64 + q * 16:jc * 64 + (q + 1) * 16],
                           inp, identb[0:16, 0:16],
                           waits=(w0 if first else []),
                           incs=[pe.bump(1)])
                    first = False
            sch.op(SCA, "copy", dtT[:], bk.t[:, 0:128],
                   waits=[pe.at()], incs=[ac.bump(1)])
            bk.release(ac)

        def conv_channel(hbuf, stride, ch, seg0, ntok, tmp):
            base = ch * stride
            for k in range(4):
                src = hbuf[:, base + seg0 + k:base + seg0 + k + ntok]
                wcol = convwb[:, ch * 4 + k:ch * 4 + k + 1]
                if k == 0:
                    sch.op(VEC, "tensor_scalar", tmp[:, 0:ntok], src, wcol,
                           None, OP.mult, waits=[(ac, ac.val)],
                           incs=[ve.bump(1)])
                else:
                    sch.op(VEC, "scalar_tensor_tensor", tmp[:, 0:ntok], src,
                           wcol, tmp[:, 0:ntok], OP.mult, OP.add,
                           incs=[ve.bump(1)])
            sch.op(SCA, "activation",
                   hbuf[:, base + seg0 + 3:base + seg0 + 3 + ntok],
                   tmp[:, 0:ntok], AF.Silu, bias=convbb[:, ch:ch + 1],
                   waits=[ve.at()], incs=[ac.bump(1)])

        def transposes_chunk(hbuf, stride, c, trc):
            for g4 in range(4):
                for jc in range(2):
                    for dst, cho in ((xhT, g4), (BT, 4 + g4)):
                        off = cho * stride + 3 + c * Q + jc * 128
                        sch.op(SYN, "dma_start_transpose",
                               dst[:, (g4 * 2 + jc) * 128:
                                   (g4 * 2 + jc + 1) * 128],
                               hbuf[:, off:off + 128],
                               waits=[ac.at(), (pe, pe.val)],
                               incs=[trc.bump(16)])

        def sschunk_state(trc):
            schist = {}
            for h in range(16):
                g4 = h // 4
                slot = h % 2
                for jc in range(2):
                    sch.op(VEC, "tensor_scalar",
                           Bsb[:, slot * 256 + jc * 128:
                               slot * 256 + (jc + 1) * 128],
                           BT[:, (g4 * 2 + jc) * 128:(g4 * 2 + jc + 1) * 128],
                           dtT[:, jc * 64 + 16 + h:jc * 64 + 17 + h], None,
                           OP.mult,
                           waits=[trc.at(), (ac, ac.val)]
                           + ([schist[h - 2]] if h >= 2 else []),
                           incs=[ve.bump(1)])
                bk = pb[6 + (h // 4) % 2]
                qoff = (h % 4) * 128
                w0 = (bk.take() if h % 4 == 0 else [])
                for jc in range(2):
                    sch.op(TEN, "matmul", bk.t[:, qoff:qoff + 128],
                           Bsb[:, slot * 256 + jc * 128:
                               slot * 256 + (jc + 1) * 128],
                           xhT[:, (g4 * 2 + jc) * 128:(g4 * 2 + jc + 1) * 128],
                           start=(jc == 0), stop=(jc == 1),
                           waits=(w0 if jc == 0 else []) + [ve.at(), trc.at()],
                           incs=[pe.bump(1)] if jc == 1 else [])
                schist[h] = (pe, pe.val)
                if h % 4 == 3:
                    for hh in range(h - 3, h + 1):
                        sch.op(VEC, "scalar_tensor_tensor",
                               S[:, hh * 128:(hh + 1) * 128],
                               S[:, hh * 128:(hh + 1) * 128],
                               dtT[:, 48 + hh:49 + hh],
                               bk.t[:, (hh % 4) * 128:(hh % 4 + 1) * 128],
                               OP.mult, OP.add,
                               waits=[pe.at(), (ac, ac.val)],
                               incs=[ve.bump(1)])
                    bk.release(ve)

        def emit_ln(rb, dst_dram, tok_slice, ab, c_row, otmp, osl, owc,
                    ofree, out_f32):
            bk1, bk2 = pb[4], pb[5]
            w1 = bk1.take() + [ve.at()]
            w2 = bk2.take() + [ac.at()]
            pehist = {}
            for chn in range(16):
                sl = np.s_[:, chn * 512:(chn + 1) * 512]
                sq = sqt[:, (chn % 2) * 512:(chn % 2 + 1) * 512]
                sch.op(SCA, "activation", sq, rb[sl], AF.Square,
                       waits=[ve.at()] + ([pehist[chn - 2]] if chn >= 2 else []),
                       incs=[ac.bump(1)])
                sch.op(TEN, "matmul", bk1.t[0:1, 0:512], onescb[:, 0:1],
                       rb[sl], start=(chn == 0), stop=(chn == 15),
                       waits=(w1 if chn == 0 else []))
                sch.op(TEN, "matmul", bk2.t[0:1, 0:512], onescb[:, 0:1], sq,
                       start=(chn == 0), stop=(chn == 15),
                       waits=(w2 if chn == 0 else []) + [ac.at()],
                       incs=[pe.bump(1)])
                pehist[chn] = (pe, pe.val)
            A = ab[0:1, 0:512]
            B = ab[0:1, 512:1024]
            sch.op(VEC, "tensor_scalar", A, bk1.t[0:1, 0:512], 1.0 / 2048.0,
                   None, OP.mult, waits=[pe.at()], incs=[ve.bump(1)])
            bk1.release(ve)
            sch.op(VEC, "tensor_scalar", B, bk2.t[0:1, 0:512], 1.0 / 2048.0,
                   None, OP.mult, incs=[ve.bump(1)])
            bk2.release(ve)
            sch.op(VEC, "tensor_tensor", c_row, A, A, OP.mult,
                   incs=[ve.bump(1)])
            sch.op(VEC, "tensor_tensor", B, B, c_row, OP.subtract,
                   incs=[ve.bump(1)])
            sch.op(SCA, "activation", B, B, AF.Sqrt, bias=epsb[0:1, 0:1],
                   waits=[ve.at()], incs=[ac.bump(1)])
            sch.op(VEC, "reciprocal", B, B, waits=[ac.at()],
                   incs=[ve.bump(1)])
            sch.op(VEC, "tensor_tensor", A, A, B, OP.mult, incs=[ve.bump(1)])
            sch.op(VEC, "tensor_scalar", A, A, -1.0, None, OP.mult,
                   incs=[ve.bump(1)])
            bkr, bkn = pb[6], pb[7]
            sch.op(TEN, "matmul", bkr.t[:, 0:512], ones1b[0:1, :], B,
                   start=True, stop=True, waits=bkr.take() + [ve.at()],
                   incs=[pe.bump(1)])
            sch.op(TEN, "matmul", bkn.t[:, 0:512], ones1b[0:1, :], A,
                   start=True, stop=True, waits=bkn.take(),
                   incs=[pe.bump(1)])
            for chn in range(16):
                sl = np.s_[:, chn * 512:(chn + 1) * 512]
                oslot = chn % 2
                t1 = otmp[:, 0:512]
                ot = osl[oslot]
                wc = owc[oslot]
                sch.op(VEC, "tensor_tensor", t1, rb[sl], bkr.t[:, 0:512],
                       OP.mult, waits=[pe.at()], incs=[ve.bump(1)])
                sch.op(VEC, "tensor_tensor", ot, t1, bkn.t[:, 0:512], OP.add,
                       waits=[ofree[oslot]], incs=[ve.bump(1)])
                sch.op(SYN, "dma_start",
                       dst_dram[chn * 128:(chn + 1) * 128, tok_slice], ot,
                       waits=[ve.at()], incs=[wc.bump(16)])
                ofree[oslot] = wc.at()
            bkr.release(ve)
            bkn.release(ve)

        esA = ExitStack()
        sbufA = lambda n, sh, dt: esA.enter_context(nc.sbuf_tensor(n, sh, dt))
        xhT = sbufA("xhT", [128, 1024], BF16)
        BT = sbufA("BT", [128, 1024], BF16)
        Bsb = sbufA("Bsb", [128, 512], BF16)
        convtmp = sbufA("convtmp", [128, 1024], F32)
        wblk = sbufA("wblk", [128, 2 * 4096], BF16)
        wdtb = sbufA("wdtb", [128, 256], BF16)
        wfree = [None, None]
        winc = [ctr["c_win0"], ctr["c_win1"]]

        def load_wblk(slot, src_dram, c0, ncol=256):
            sch.op(GPS, "dma_start",
                   wblk[:, slot * 4096:slot * 4096 + 16 * ncol]
                   .rearrange("p (k c) -> p k c", k=16),
                   src_dram[:, c0:c0 + ncol]
                   .rearrange("(k p) c -> p k c", p=128),
                   waits=[wfree[slot]], incs=[winc[slot].bump(16)])

        def wtile(slot, kc, sub):
            base = slot * 4096 + kc * 256 + sub * 128
            return wblk[:, base:base + 128]

        esR = ExitStack()
        xpreb = esR.enter_context(nc.sbuf_tensor("xpreb", [128, 16 * T], BF16))
        hbcpre = esR.enter_context(
            nc.sbuf_tensor("hbcpre", [128, 8 * PREW], BF16))
        cx = ctr["c_xg"]
        sch.op(GPS, "dma_start", xpreb.rearrange("p (k t) -> p k t", k=16),
               xpre_fm.rearrange("(k p) t -> p k t", p=128),
               incs=[cx.bump(16)])
        for kc in range(16):
            sch.op(VEC, "tensor_copy", xpreh[:, kc * 3:(kc + 1) * 3],
                   xpreb[:, kc * T + 2045:kc * T + 2048],
                   waits=[cx.at()], incs=[ve.bump(1)])

        for b in range(4):
            slot = b % 2
            load_wblk(slot, w_in, 2048 + b * 256)
            for sub in range(2):
                ch = b * 2 + sub
                for tt in range(4):
                    bk = pb[(b * 8 + sub * 4 + tt) % 4]
                    w0 = bk.take() + [winc[slot].at(), cx.at()]
                    for kc in range(16):
                        sch.op(TEN, "matmul", bk.t[:], wtile(slot, kc, sub),
                               xpreb[:, kc * T + tt * 512:
                                     kc * T + (tt + 1) * 512],
                               start=(kc == 0), stop=(kc == 15),
                               waits=(w0 if kc == 0 else []),
                               incs=[pe.bump(1)] if kc == 15 else [])
                    dst = hbcpre[:, ch * PREW + 3 + tt * 512:
                                 ch * PREW + 3 + (tt + 1) * 512]
                    sch.op(VEC, "tensor_copy", dst, bk.t[:], waits=[pe.at()],
                           incs=[ve.bump(1)])
                    bk.release(ve)
            wfree[slot] = pe.at()
        sch.op(GPS, "dma_start", wdtb.rearrange("p (k c) -> p k c", k=16),
               w_in[:, 5120:5136].rearrange("(k p) c -> p k c", p=128),
               waits=[wfree[0]], incs=[winc[0].bump(16)])
        for tt in range(4):
            bk = pb[tt % 4]
            w0 = bk.take() + [winc[0].at()]
            for kc in range(16):
                sch.op(TEN, "matmul", bk.t[0:16, 0:512],
                       wdtb[:, kc * 16:(kc + 1) * 16],
                       xpreb[:, kc * T + tt * 512:kc * T + (tt + 1) * 512],
                       start=(kc == 0), stop=(kc == 15),
                       waits=(w0 if kc == 0 else []),
                       incs=[pe.bump(1)] if kc == 15 else [])
            dt_block_evac(bk, np.s_[tt * 512:(tt + 1) * 512],
                          dte1[:], dte2[:])
        wfree[0] = pe.at()

        for ch in range(8):
            sch.op(VEC, "tensor_copy", halo24[:, ch * 3:ch * 3 + 3],
                   hbcpre[:, ch * PREW + 3 + 2045:ch * PREW + 3 + 2048],
                   waits=[ac.at()], incs=[ve.bump(1)])
        for ch in range(8):
            sch.op(VEC, "memset", hbcpre[:, ch * PREW:ch * PREW + 3], 0.0,
                   incs=[ve.bump(1)])
        for ch in range(8):
            conv_channel(hbcpre, PREW, ch, 1024, 1024, convtmp)
            conv_channel(hbcpre, PREW, ch, 0, 1024, convtmp)
        if DBG:
            sch.op(SYN, "dma_start", dbg_rdt[:, :], dtfull[:],
                   waits=[ac.at()], incs=[ctr["c_ow0"].bump(16)])
            sch.op(SYN, "dma_start", dbg_pre[:, :], hbcpre[:],
                   waits=[ac.at()], incs=[ctr["c_ow0"].bump(16)])
        for c in range(8):
            dt_chunk(c * Q, need_scores=False)
            if DBG and c == 0:
                sch.op(SYN, "dma_start", dbg_dtt[:, :], dtT[:],
                       waits=[ve.at()], incs=[ctr["c_ow0"].bump(16)])
                sch.op(SYN, "dma_start", dbg_dld[:, 0:256], dldtch[:],
                       waits=[ve.at()], incs=[ctr["c_ow0"].bump(16)])
                sch.op(SYN, "dma_start", dbg_dld[:, 256:512], cumch[:],
                       waits=[ve.at()], incs=[ctr["c_ow0"].bump(16)])
                sch.op(SYN, "dma_start", dbg_dld[:, 512:768], ecumch[:],
                       waits=[ve.at()], incs=[ctr["c_ow0"].bump(16)])
                sch.op(SYN, "dma_start", dbg_dld[:, 768:1024], dach[:],
                       waits=[ve.at()], incs=[ctr["c_ow0"].bump(16)])
            trc = ctr["c_tr0"] if c % 2 == 0 else ctr["c_tr1"]
            transposes_chunk(hbcpre, PREW, c, trc)
            sschunk_state(trc)
            if DBG and c == 0:
                sch.op(SYN, "dma_start", dbg_xh[:, :], xhT[:],
                       waits=[trc.at()], incs=[ctr["c_ow0"].bump(16)])
                sch.op(SYN, "dma_start", dbg_bt[:, :], BT[:],
                       waits=[trc.at()], incs=[ctr["c_ow0"].bump(16)])
                sch.op(SYN, "dma_start", dbg_s2[:, :], S[:],
                       waits=[ve.at()], incs=[ctr["c_ow0"].bump(16)])
        if DBG and False:
            sch.op(SYN, "dma_start", dbg_s2[:, :], S[:],
                   waits=[ve.at()], incs=[ctr["c_ow0"].bump(16)])
        sch.op(VEC, "tensor_scalar", S[:], S[:], s0selb[:, 0:1], None,
               OP.mult, waits=[ctr["c_ow0"].at()] if DBG else [],
               incs=[ve.bump(1)])
        if DBG:
            sch.op(SYN, "dma_start", dbg_s[:, :], S[:], waits=[ve.at()],
                   incs=[ctr["c_ow0"].bump(16)])
        esR.close()
        sch.barrier(ctr["bar"])

        esG = ExitStack()
        sbufG = lambda n, sh, dt: esG.enter_context(nc.sbuf_tensor(n, sh, dt))
        S_bf = sbufG("S_bf", [128, NH * 128], BF16)
        DIb = sbufG("DIb", [128, 16 * 128], BF16)
        sdf = sbufG("sdf", [128, 512], F32)
        sxp = sbufG("sxp", [128, 512], BF16)
        mpb = sbufG("mpb", [128, 2 * 512], BF16)
        ctb = sbufG("ctb", [128, 2 * 256], BF16)
        gpre = sbufG("gpre", [128, 16 * 256], BF16)
        g2q = sbufG("g2q", [128, 2 * 256], BF16)
        x1o = sbufG("x1o", [128, 2 * 512], BF16)
        hbc = sbufG("hbc", [128, 24 * HBW], BF16)
        x1wc = [ctr["c_x1w0"], ctr["c_x1w1"]]

        for h in range(16):
            sch.op(VEC, "tensor_scalar", DIb[:, h * 128:(h + 1) * 128],
                   identbf[:], drepb[:, h:h + 1], None, OP.mult,
                   incs=[ve.bump(1)])

        xg_free = None
        x1free = [None, None]
        for g in range(2):
            esGg = ExitStack()
            gateb = esGg.enter_context(
                nc.sbuf_tensor(f"gate{g}", [128, 16 * TG], BF16))
            esXW = ExitStack()
            xgb = esXW.enter_context(
                nc.sbuf_tensor(f"xg{g}", [128, 16 * TG], BF16))
            sch.op(GPS, "dma_start", xgb.rearrange("p (k t) -> p k t", k=16),
                   x_fm[:, g * TG:(g + 1) * TG]
                   .rearrange("(k p) t -> p k t", p=128),
                   waits=[xg_free], incs=[cx.bump(16)])

            col_blocks = ([("hbc", 2048 + b * 256, b) for b in range(12)]
                          + [("dt", 5120, 0)]
                          + [("gate", b * 256, b) for b in range(8)])
            bkrot = 0
            for kind, c0, bidx in col_blocks:
                if kind == "dt":
                    sch.op(GPS, "dma_start",
                           wdtb.rearrange("p (k c) -> p k c", k=16),
                           w_in[:, 5120:5136]
                           .rearrange("(k p) c -> p k c", p=128),
                           waits=[wfree[0]], incs=[winc[0].bump(16)])
                    for tt in range(2):
                        bk = pb[bkrot % 4]; bkrot += 1
                        w0 = bk.take() + [winc[0].at(), cx.at()]
                        for kc in range(16):
                            sch.op(TEN, "matmul", bk.t[0:16, 0:512],
                                   wdtb[:, kc * 16:(kc + 1) * 16],
                                   xgb[:, kc * TG + tt * 512:
                                       kc * TG + (tt + 1) * 512],
                                   start=(kc == 0), stop=(kc == 15),
                                   waits=(w0 if kc == 0 else []),
                                   incs=[pe.bump(1)] if kc == 15 else [])
                        dt_block_evac(bk, np.s_[tt * 512:(tt + 1) * 512],
                                  dte1[:], dte2[:])
                    wfree[0] = pe.at()
                    continue
                slot = bidx % 2
                load_wblk(slot, w_in if kind == "hbc" else w_in, c0)
                for sub in range(2):
                    for tt in range(2):
                        bk = pb[bkrot % 4]; bkrot += 1
                        w0 = bk.take() + [winc[slot].at(), cx.at()]
                        for kc in range(16):
                            sch.op(TEN, "matmul", bk.t[:],
                                   wtile(slot, kc, sub),
                                   xgb[:, kc * TG + tt * 512:
                                       kc * TG + (tt + 1) * 512],
                                   start=(kc == 0), stop=(kc == 15),
                                   waits=(w0 if kc == 0 else []),
                                   incs=[pe.bump(1)] if kc == 15 else [])
                        if kind == "hbc":
                            ch = bidx * 2 + sub
                            dst = hbc[:, ch * HBW + 3 + tt * 512:
                                      ch * HBW + 3 + (tt + 1) * 512]
                            sch.op(VEC, "tensor_copy", dst, bk.t[:],
                                   waits=[pe.at()], incs=[ve.bump(1)])
                            bk.release(ve)
                        else:
                            ch = bidx * 2 + sub
                            dst = gateb[:, ch * TG + tt * 512:
                                        ch * TG + (tt + 1) * 512]
                            sch.op(SCA, "activation", dst, bk.t[:], AF.Silu,
                                   waits=[pe.at()], incs=[ac.bump(1)])
                            bk.release(ac)
                    if g == 0 and kind == "hbc" and bidx >= 4:
                        ch = bidx * 2 + sub
                        bk = pb[4]
                        w0 = bk.take()
                        for kc in range(16):
                            sch.op(TEN, "matmul", bk.t[:, 0:3],
                                   wtile(slot, kc, sub),
                                   xpreh[:, kc * 3:(kc + 1) * 3],
                                   start=(kc == 0), stop=(kc == 15),
                                   waits=(w0 if kc == 0 else []),
                                   incs=[pe.bump(1)] if kc == 15 else [])
                        sch.op(VEC, "tensor_copy",
                               hbc[:, ch * HBW:ch * HBW + 3], bk.t[:, 0:3],
                               waits=[pe.at()], incs=[ve.bump(1)])
                        bk.release(ve)
                if kind == "hbc":
                    for sub in range(2):
                        ch = bidx * 2 + sub
                        if g == 1 or ch < 8:
                            sch.op(VEC, "tensor_copy",
                                   hbc[:, ch * HBW:ch * HBW + 3],
                                   halo24[:, ch * 3:ch * 3 + 3],
                                   incs=[ve.bump(1)])
                        if g == 0:
                            sch.op(VEC, "tensor_copy",
                                   halo24[:, ch * 3:ch * 3 + 3],
                                   hbc[:, ch * HBW + 3 + TG - 3:
                                       ch * HBW + 3 + TG],
                                   incs=[ve.bump(1)])
                        conv_channel(hbc, HBW, ch, 0, TG, convtmp)
                wfree[slot] = pe.at()
            esXW.close()
            xg_free = pe.at()
            if DBG and g == 0:
                sch.op(SYN, "dma_start", dbg_hbc[:, :], hbc[:],
                       waits=[ac.at()], incs=[ctr["c_ow0"].bump(16)])
                sch.op(SYN, "dma_start", dbg_dt[:, 0:TG], dtfull[:, 0:TG],
                       waits=[ac.at()], incs=[ctr["c_ow0"].bump(16)])

            for c in range(4):
                sch.op(VEC, "tensor_copy", S_bf[:], S[:],
                       waits=[(pe, pe.val)], incs=[ve.bump(1)])
                dt_chunk(c * Q, need_scores=True)
                trc = ctr["c_tr0"] if c % 2 == 0 else ctr["c_tr1"]
                transposes_chunk(hbc, HBW, c, trc)
                yhist = {}
                sc2hist = {}
                for h in range(16):
                    g4 = h // 4
                    slot = h % 2
                    sell = selb[:, h * 128:(h + 1) * 128]
                    bkbe = pb[4 + slot]
                    w0 = bkbe.take() + [ve.at()]
                    for i in range(3):
                        sch.op(TEN, "matmul", bkbe.t[:, 0:256], sell,
                               cumsp[:, i * Q:(i + 1) * Q],
                               start=(i == 0), stop=(i == 2),
                               waits=(w0 if i == 0 else []))
                    for i in range(2):
                        sch.op(TEN, "matmul", bkbe.t[:, 256:512], sell,
                               ecusp[:, i * Q:(i + 1) * Q],
                               start=(i == 0), stop=(i == 1),
                               incs=[pe.bump(1)] if i == 1 else [])
                    bks = pb[slot]
                    w0 = bks.take()
                    for jc in range(2):
                        sch.op(TEN, "matmul",
                               bks.t[:, jc * 256:(jc + 1) * 256],
                               hbc[:, (4 + g4) * HBW + 3 + c * Q + jc * 128:
                                   (4 + g4) * HBW + 3 + c * Q + (jc + 1) * 128],
                               hbc[:, (8 + h) * HBW + 3 + c * Q:
                                   (8 + h) * HBW + 3 + (c + 1) * Q],
                               start=True, stop=True,
                               waits=(w0 if jc == 0 else []),
                               incs=[pe.bump(1)] if jc == 1 else [])
                    for jc in range(2):
                        sch.op(VEC, "scalar_tensor_tensor",
                               sdf[:, jc * 256:(jc + 1) * 256],
                               bkbe.t[:, 0:256],
                               dtT[:, jc * 64 + h:jc * 64 + h + 1],
                               maskb[:, jc * 256:(jc + 1) * 256],
                               OP.subtract, OP.add,
                               waits=[pe.at(), (ac, ac.val)],
                               incs=[ve.bump(1)])
                    sch.op(SCA, "activation", sxp[:], sdf[:], AF.Exp,
                           waits=[ve.at()], incs=[ac.bump(1)])
                    for jc in range(2):
                        sch.op(VEC, "scalar_tensor_tensor",
                               mpb[:, slot * 512 + jc * 256:
                                   slot * 512 + (jc + 1) * 256],
                               bks.t[:, jc * 256:(jc + 1) * 256],
                               dtT[:, jc * 64 + 32 + h:jc * 64 + 33 + h],
                               sxp[:, jc * 256:(jc + 1) * 256],
                               OP.mult, OP.mult,
                               waits=[ac.at()] + ([yhist[h - 2]]
                                                  if h >= 2 else []),
                               incs=[ve.bump(1)])
                    bks.release(ve)
                    sch.op(VEC, "tensor_tensor",
                           ctb[:, slot * 256:(slot + 1) * 256],
                           hbc[:, (8 + h) * HBW + 3 + c * Q:
                               (8 + h) * HBW + 3 + (c + 1) * Q],
                           bkbe.t[:, 256:512], OP.mult, incs=[ve.bump(1)])
                    bkbe.release(ve)
                    bky = pb[2 + slot]
                    w0 = bky.take()
                    for jc in range(2):
                        sch.op(TEN, "matmul", bky.t[:, 0:256],
                               xhT[:, (g4 * 2 + jc) * 128:
                                   (g4 * 2 + jc + 1) * 128],
                               mpb[:, slot * 512 + jc * 256:
                                   slot * 512 + (jc + 1) * 256],
                               start=(jc == 0), stop=False,
                               waits=(w0 if jc == 0 else []) + [ve.at(),
                                                                trc.at()])
                    sch.op(TEN, "matmul", bky.t[:, 0:256],
                           DIb[:, h * 128:(h + 1) * 128],
                           hbc[:, g4 * HBW + 3 + c * Q:
                               g4 * HBW + 3 + (c + 1) * Q],
                           start=False, stop=False)
                    sch.op(TEN, "matmul", bky.t[:, 0:256],
                           S_bf[:, h * 128:(h + 1) * 128],
                           ctb[:, slot * 256:(slot + 1) * 256],
                           start=False, stop=True, incs=[pe.bump(1)])
                    yhist[h] = (pe, pe.val)
                    sch.op(VEC, "tensor_tensor",
                           gpre[:, h * 256:(h + 1) * 256], bky.t[:, 0:256],
                           gateb[:, h * TG + c * Q:h * TG + (c + 1) * Q],
                           OP.mult, waits=[pe.at(), (ac, ac.val)],
                           incs=[ve.bump(1)])
                    bky.release(ve)
                    for jc in range(2):
                        sch.op(VEC, "tensor_scalar",
                               Bsb[:, slot * 256 + jc * 128:
                                   slot * 256 + (jc + 1) * 128],
                               BT[:, (g4 * 2 + jc) * 128:
                                   (g4 * 2 + jc + 1) * 128],
                               dtT[:, jc * 64 + 16 + h:jc * 64 + 17 + h],
                               None, OP.mult,
                               waits=[(ac, ac.val)]
                               + ([sc2hist[h - 2]] if h >= 2 else []),
                               incs=[ve.bump(1)])
                    bksc = pb[6 + (h // 4) % 2]
                    qoff = (h % 4) * 128
                    w0 = (bksc.take() if h % 4 == 0 else [])
                    for jc in range(2):
                        sch.op(TEN, "matmul", bksc.t[:, qoff:qoff + 128],
                               Bsb[:, slot * 256 + jc * 128:
                                   slot * 256 + (jc + 1) * 128],
                               xhT[:, (g4 * 2 + jc) * 128:
                                   (g4 * 2 + jc + 1) * 128],
                               start=(jc == 0), stop=(jc == 1),
                               waits=(w0 if jc == 0 else []) + [ve.at()],
                               incs=[pe.bump(1)] if jc == 1 else [])
                    sc2hist[h] = (pe, pe.val)
                    if h % 4 == 3:
                        for hh in range(h - 3, h + 1):
                            sch.op(VEC, "scalar_tensor_tensor",
                                   S[:, hh * 128:(hh + 1) * 128],
                                   S[:, hh * 128:(hh + 1) * 128],
                                   dtT[:, 48 + hh:49 + hh],
                                   bksc.t[:, (hh % 4) * 128:
                                          (hh % 4 + 1) * 128],
                                   OP.mult, OP.add,
                                   waits=[pe.at(), (ac, ac.val)],
                                   incs=[ve.bump(1)])
                        bksc.release(ve)
                bkgn = pb[4]
                wgn = bkgn.take()
                gn_hist = {}
                for h in range(16):
                    sq = g2q[:, (h % 2) * 256:(h % 2 + 1) * 256]
                    sch.op(SCA, "activation", sq,
                           gpre[:, h * 256:(h + 1) * 256], AF.Square,
                           waits=[ve.at()] + ([gn_hist[h - 2]]
                                              if h >= 2 else []),
                           incs=[ac.bump(1)])
                    sch.op(TEN, "matmul", bkgn.t[0:16, 0:256],
                           ohb[:, h * 16:(h + 1) * 16], sq,
                           start=(h == 0), stop=(h == 15),
                           waits=(wgn if h == 0 else []) + [ac.at()],
                           incs=[pe.bump(1)])
                    gn_hist[h] = (pe, pe.val)
                sch.op(SCA, "activation", gnrs[:], bkgn.t[0:16, 0:256],
                       AF.Sqrt, bias=epsb[0:16, 0:1],
                       scale=scl128b[0:16, 0:1],
                       waits=[pe.at()], incs=[ac.bump(1)])
                bkgn.release(ac)
                sch.op(VEC, "reciprocal", gnrs[:], gnrs[:], waits=[ac.at()],
                       incs=[ve.bump(1)])
                split2(rssp, gnrs[:])
                for h in range(16):
                    bkr = pb[5 + (h % 2) * 2]
                    sell = selb[:, h * 128:(h + 1) * 128]
                    w0 = bkr.take() + [ve.at()]
                    for i in range(2):
                        sch.op(TEN, "matmul", bkr.t[:, 0:256], sell,
                               rssp[:, i * Q:(i + 1) * Q],
                               start=(i == 0), stop=(i == 1),
                               waits=(w0 if i == 0 else []),
                               incs=[pe.bump(1)] if i == 1 else [])
                    sch.op(VEC, "scalar_tensor_tensor",
                           hbc[:, (8 + h) * HBW + 3 + c * Q:
                               (8 + h) * HBW + 3 + (c + 1) * Q],
                           bkr.t[:, 0:256], nwstb[:, h:h + 1],
                           gpre[:, h * 256:(h + 1) * 256],
                           OP.mult, OP.mult, waits=[pe.at()],
                           incs=[ve.bump(1)])
                    bkr.release(ve)
                if DBG and g == 0 and c == 0:
                    sch.op(SYN, "dma_start", dbg_g[:, :], gpre[:],
                           waits=[ve.at()], incs=[ctr["c_ow0"].bump(16)])
            esGg.close()

            esO = ExitStack()
            r1b = esO.enter_context(
                nc.sbuf_tensor(f"r1_{g}", [128, 16 * 512], BF16))
            xresb = esO.enter_context(
                nc.sbuf_tensor(f"xres_{g}", [128, 16 * 512], BF16))
            for tt in range(2):
                sch.op(GPS, "dma_start",
                       xresb.rearrange("p (k t) -> p k t", k=16),
                       x_fm[:, g * TG + tt * 512:g * TG + (tt + 1) * 512]
                       .rearrange("(k p) t -> p k t", p=128),
                       waits=[(pe, pe.val), (ve, ve.val)],
                       incs=[cx.bump(16)])
                for dblk in range(8):
                    slot = dblk % 2
                    load_wblk(slot, w_out, dblk * 256)
                    for sub in range(2):
                        dcol = dblk * 2 + sub
                        bk = pb[bkrot % 4]; bkrot += 1
                        w0 = bk.take() + [winc[slot].at(), ve.at(), cx.at()]
                        for kc in range(16):
                            sch.op(TEN, "matmul", bk.t[:],
                                   wtile(slot, kc, sub),
                                   hbc[:, (8 + kc) * HBW + 3 + tt * 512:
                                       (8 + kc) * HBW + 3 + (tt + 1) * 512],
                                   start=(kc == 0), stop=(kc == 15),
                                   waits=(w0 if kc == 0 else []),
                                   incs=[pe.bump(1)] if kc == 15 else [])
                        sch.op(VEC, "scalar_tensor_tensor",
                               r1b[:, dcol * 512:(dcol + 1) * 512],
                               bk.t[:], 1.0,
                               xresb[:, dcol * 512:(dcol + 1) * 512],
                               OP.mult, OP.add, waits=[pe.at(), cx.at()],
                               incs=[ve.bump(1)])
                        bk.release(ve)
                    wfree[slot] = pe.at()
                emit_ln(r1b, x1_scr,
                        np.s_[g * TG + tt * 512:g * TG + (tt + 1) * 512],
                        convtmp, crowb[:], convtmp,
                        [x1o[:, 0:512], x1o[:, 512:1024]], x1wc, x1free,
                        out_f32=False)
            esO.close()
        esG.close()
        esA.close()
        sch.op(SYN, "wait_ge", x1wc[0].sem, x1wc[0].val)
        sch.op(SYN, "wait_ge", x1wc[1].sem, x1wc[1].val)
        sch.barrier(ctr["bar"])

        esF2 = ExitStack()
        x1f = esF2.enter_context(nc.sbuf_tensor("x1f", [128, 16 * T], BF16))
        wupf = esF2.enter_context(nc.sbuf_tensor("wupf", [128, 2 * 2048], F32))
        wupb = esF2.enter_context(nc.sbuf_tensor("wupb", [128, 2 * 2048], BF16))
        hidt = esF2.enter_context(nc.sbuf_tensor("hidt", [128, 4 * 512], BF16))
        cl = ctr["c_ldr"]
        sch.op(SYN, "dma_start", x1f.rearrange("p (k t) -> p k t", k=16),
               x1_scr.rearrange("(k p) t -> p k t", p=128),
               waits=[x1wc[0].at(), x1wc[1].at()], incs=[cl.bump(16)])
        wupc = [ctr["c_wup0"], ctr["c_wup1"]]
        upfree = [None, None]
        upbfree = [None, None]
        hidfree = {0: None, 1: None, 2: None, 3: None}
        hwc = [ctr["c_hw0"], ctr["c_hw1"], ctr["c_hw2"], ctr["c_hw3"]]
        for hcol in range(64):
            slot = hcol % 2
            sch.op(SYN, "dma_start",
                   wupf[:, slot * 2048:(slot + 1) * 2048]
                   .rearrange("p (k c) -> p k c", k=16),
                   w_up_t[:, hcol * 128:(hcol + 1) * 128]
                   .rearrange("(k p) c -> p k c", p=128),
                   waits=[upfree[slot]], incs=[wupc[slot].bump(16)])
            sch.op(SCA, "sign", wupb[:, slot * 2048:(slot + 1) * 2048],
                   wupf[:, slot * 2048:(slot + 1) * 2048],
                   waits=[wupc[slot].at(), upbfree[slot]],
                   incs=[ac.bump(1)])
            upfree[slot] = ac.at()
            for tt in range(4):
                bk = pb[tt % 4]
                w0 = bk.take() + [ac.at(), cl.at()]
                for kc in range(16):
                    sch.op(TEN, "matmul", bk.t[:],
                           wupb[:, slot * 2048 + kc * 128:
                                slot * 2048 + (kc + 1) * 128],
                           x1f[:, kc * T + tt * 512:kc * T + (tt + 1) * 512],
                           start=(kc == 0), stop=(kc == 15),
                           waits=(w0 if kc == 0 else []),
                           incs=[pe.bump(1)] if kc == 15 else [])
                ht = hidt[:, tt * 512:(tt + 1) * 512]
                sch.op(SCA, "activation", ht, bk.t[:], AF.Relu,
                       scale=upscb[:, hcol:hcol + 1],
                       waits=[pe.at(), hidfree[tt]], incs=[ac.bump(1)])
                bk.release(ac)
                sch.op(GPS, "dma_start",
                       hid_scr[hcol * 128:(hcol + 1) * 128,
                               tt * 512:(tt + 1) * 512],
                       ht, waits=[ac.at()], incs=[hwc[tt].bump(16)])
                hidfree[tt] = hwc[tt].at()
            upbfree[slot] = (pe, pe.val)
        esF2.close()
        for i in range(4):
            sch.op(GPS, "wait_ge", hwc[i].sem, hwc[i].val)
        sch.barrier(ctr["bar"])

        esF3 = ExitStack()
        hidf = esF3.enter_context(nc.sbuf_tensor("hidf", [128, 64 * 512], BF16))
        x1k = esF3.enter_context(nc.sbuf_tensor("x1k", [128, 16 * 512], BF16))
        wdnf = esF3.enter_context(nc.sbuf_tensor("wdnf", [128, 2 * 2048], F32))
        wdnb = esF3.enter_context(nc.sbuf_tensor("wdnb", [128, 2 * 8192], BF16))
        r2b = esF3.enter_context(nc.sbuf_tensor("r2b", [128, 16 * 512], BF16))
        outt = esF3.enter_context(nc.sbuf_tensor("outt", [128, 2 * 512], F32))
        f3tmp = esF3.enter_context(nc.sbuf_tensor("f3tmp", [128, 1024], F32))
        wdnc = [ctr["c_wdn0"], ctr["c_wdn1"]]
        cl2 = ctr["c_ldr2"]
        wdnw = ctr["c_x1w0"]
        dnfree = [None, None]
        owc = [ctr["c_ow0"], ctr["c_ow1"]]
        outfree = [None, None]
        dn_hist = {}
        wdnw_hist = {}
        for tt in range(4):
            sch.op(GPS, "dma_start", hidf.rearrange("p (k t) -> p k t", k=64),
                   hid_scr[:, tt * 512:(tt + 1) * 512]
                   .rearrange("(k p) t -> p k t", p=128),
                   waits=[hwc[0].at(), hwc[1].at(), hwc[2].at(), hwc[3].at(),
                          (pe, pe.val)],
                   incs=[cl2.bump(16)])
            sch.op(GPS, "dma_start", x1k.rearrange("p (k t) -> p k t", k=16),
                   x1_scr[:, tt * 512:(tt + 1) * 512]
                   .rearrange("(k p) t -> p k t", p=128),
                   waits=[(ve, ve.val)], incs=[cl2.bump(16)])
            for dcol in range(16):
                wslot = dcol % 2
                if tt == 0:
                    for half in range(4):
                        hs = half % 2
                        sch.op(SYN, "dma_start",
                               wdnf[:, hs * 2048:(hs + 1) * 2048]
                               .rearrange("p (k c) -> p k c", k=16),
                               w_dn_t[half * 2048:(half + 1) * 2048,
                                      dcol * 128:(dcol + 1) * 128]
                               .rearrange("(k p) c -> p k c", p=128),
                               waits=[dnfree[hs]], incs=[wdnc[hs].bump(16)])
                        sch.op(SCA, "sign",
                               wdnb[:, wslot * 8192 + half * 2048:
                                    wslot * 8192 + (half + 1) * 2048],
                               wdnf[:, hs * 2048:(hs + 1) * 2048],
                               waits=[wdnc[hs].at(),
                                      dn_hist.get((tt * 16 + dcol) - 2),
                                      wdnw_hist.get(dcol - 2)],
                               incs=[ac.bump(1)])
                        dnfree[hs] = ac.at()
                    sch.op(SYN, "dma_start",
                           wdn_scr[dcol * 128:(dcol + 1) * 128, :],
                           wdnb[:, wslot * 8192:(wslot + 1) * 8192],
                           waits=[(ac, ac.val)], incs=[wdnw.bump(16)])
                    wdnw_hist[dcol] = wdnw.at()
                else:
                    sch.op(SYN, "dma_start",
                           wdnb[:, wslot * 8192:(wslot + 1) * 8192],
                           wdn_scr[dcol * 128:(dcol + 1) * 128, :],
                           waits=[wdnw.at(),
                                  dn_hist.get((tt * 16 + dcol) - 2)],
                           incs=[wdnc[0].bump(16)])
                bk = pb[dcol % 4]
                w0 = bk.take() + ([ac.at()] if tt == 0
                                  else [wdnc[0].at()]) + [cl2.at()]
                for hk in range(64):
                    sch.op(TEN, "matmul", bk.t[:],
                           wdnb[:, wslot * 8192 + hk * 128:
                                wslot * 8192 + (hk + 1) * 128],
                           hidf[:, hk * 512:(hk + 1) * 512],
                           start=(hk == 0), stop=(hk == 63),
                           waits=(w0 if hk == 0 else []),
                           incs=[pe.bump(1)] if hk == 63 else [])
                dn_hist[tt * 16 + dcol] = (pe, pe.val)
                sch.op(VEC, "scalar_tensor_tensor",
                       r2b[:, dcol * 512:(dcol + 1) * 512], bk.t[:],
                       dsob[:, dcol:dcol + 1],
                       x1k[:, dcol * 512:(dcol + 1) * 512],
                       OP.mult, OP.add, waits=[pe.at(), cl2.at()],
                       incs=[ve.bump(1)])
                bk.release(ve)
            emit_ln(r2b, out_fm, np.s_[tt * 512:(tt + 1) * 512],
                    f3tmp, crowb[:], f3tmp,
                    [outt[:, 0:512], outt[:, 512:1024]], owc, outfree,
                    out_f32=True)
        esF3.close()
        sch.op(SYN, "wait_ge", owc[0].sem, owc[0].val)
        sch.op(SYN, "wait_ge", owc[1].sem, owc[1].val)

        with nc.Block() as block:
            @block.tensor
            def _(e):
                sch.replay(TEN, e)

            @block.vector
            def _(e):
                sch.replay(VEC, e)

            @block.scalar
            def _(e):
                sch.replay(SCA, e)

            @block.gpsimd
            def _(e):
                sch.replay(GPS, e)

            @block.sync
            def _(e):
                sch.replay(SYN, e)

    return nc


_NC_CACHE = {}


def kernel(L0_attn_hidden_states, L0_attn_W_in, L0_attn_conv_weight,
           L0_attn_conv_bias, L0_attn_A_log, L0_attn_D, L0_attn_dt_bias,
           L0_attn_norm_weight, L0_attn_W_out, L0_ffn_up_weight,
           L0_ffn_down_weight, L0_ffn_up_scale, L0_ffn_down_scale,
           L0_ffn_output_scale):
    hs = np.asarray(L0_attn_hidden_states, np.float32)
    B, L, _ = hs.shape
    f32c = lambda a: np.ascontiguousarray(np.asarray(a, np.float32))

    shared = {
        "w_in": f32c(L0_attn_W_in),
        "w_out": f32c(L0_attn_W_out),
        "w_up_t": f32c(np.asarray(L0_ffn_up_weight).T),
        "w_dn_t": f32c(np.asarray(L0_ffn_down_weight).T),
        "conv_w": f32c(np.asarray(L0_attn_conv_weight)
                       .reshape(24, 128, 4).transpose(1, 0, 2)
                       .reshape(128, 96)),
        "conv_b": f32c(np.asarray(L0_attn_conv_bias).reshape(24, 128).T),
        "alog_d": f32c(np.asarray(L0_attn_A_log).reshape(NH, 1)),
        "dtb_d": f32c(np.asarray(L0_attn_dt_bias).reshape(NH, 1)),
        "drep_d": f32c(np.broadcast_to(np.asarray(L0_attn_D).reshape(1, NH),
                                       (128, NH))),
        "nwst_d": f32c(np.asarray(L0_attn_norm_weight).reshape(16, 128).T),
        "upsc_d": f32c(np.asarray(L0_ffn_up_scale).reshape(64, 128).T),
        "dnsc_d": f32c(np.asarray(L0_ffn_down_scale).reshape(16, 128).T),
        "outsc_d": f32c(np.broadcast_to(
            np.asarray(L0_ffn_output_scale).reshape(1, 1), (128, 1))),
    }

    in_maps = []
    for core in range(8):
        b, half = core // 2, core % 2
        m = dict(shared)
        m["x_fm"] = f32c(hs[b, half * T:(half + 1) * T, :].T)
        m["xpre_fm"] = (f32c(hs[b, 0:T, :].T) if half == 1
                        else np.zeros((HID, T), np.float32))
        m["s0sel_d"] = np.full((128, 1), float(half), np.float32)
        in_maps.append(m)

    if "v1" not in _NC_CACHE:
        _NC_CACHE["v1"] = build_nc()
    nc = _NC_CACHE["v1"]

    res = run_bass_kernel_spmd(nc, in_maps, core_ids=list(range(8)))

    out = np.empty((B, L, HID), np.float32)
    for core in range(8):
        b, half = core // 2, core % 2
        out[b, half * T:(half + 1) * T, :] = res.results[core]["out_fm"].T
    if DBG:
        kernel._dbg = res.results
    return out


# version 11 (final)
# speedup vs baseline: 1.0295x; 1.0295x over previous
import numpy as np
import ml_dtypes
from contextlib import ExitStack

import concourse.bass as bass
import concourse.mybir as mybir
from concourse.bass_utils import run_bass_kernel_spmd
from concourse.alu_op_type import AluOpType as OP

F32 = mybir.dt.float32
BF16 = mybir.dt.bfloat16
AF = mybir.ActivationFunctionType

TEN, VEC, SCA, GPS, SYN = "tensor", "vector", "scalar", "gpsimd", "sync"
ENGS = (TEN, VEC, SCA, GPS, SYN)

HID = 2048
NH = 16
Q = 256
EPS = 1e-5
T = 2048
TG = 1024
HBW = TG + 3
PREW = T + 3

DBG = False


class Ctr:
    def __init__(self, sem, owner=None):
        self.sem = sem
        self.val = 0
        self.owner = owner

    def bump(self, n):
        self.val += n
        return (self, n)

    def at(self, val=None):
        return (self, self.val if val is None else val)


class Sched:
    def __init__(self):
        self.q = {e: [] for e in ENGS}
        self.waited = {e: {} for e in ENGS}

    def op(self, eng, method, *args, waits=(), incs=(), **kwargs):
        ws = []
        for w in waits:
            if w is None:
                continue
            c, val = w
            if c.owner == eng or val <= 0:
                continue
            key = c.sem.num
            if self.waited[eng].get(key, -1) >= val:
                continue
            self.waited[eng][key] = val
            ws.append((c.sem, val))
        ics = []
        for c, n in incs:
            ics.append((c.sem, n))
        self.q[eng].append((method, args, kwargs, ws, ics))

    def barrier(self, bctr):
        target = bctr.val + 5
        for e in ENGS:
            self.q[e].append(("sem_inc", (bctr.sem, 1), {}, [], []))
        bctr.val = target
        for e in ENGS:
            self.q[e].append(("wait_ge", (bctr.sem, target), {}, [], []))
            self.waited[e][bctr.sem.num] = target

    def replay(self, eng_name, eng):
        for method, args, kwargs, ws, ics in self.q[eng_name]:
            for sem, val in ws:
                eng.wait_ge(sem, val)
            inst = getattr(eng, method)(*args, **kwargs)
            for sem, n in ics:
                inst.then_inc(sem, n, skip_validation=True)


class Bank:
    def __init__(self, t):
        self.t = t
        self.pending = []

    def take(self):
        ws = self.pending
        self.pending = []
        return ws

    def release(self, ctr):
        self.pending.append(ctr.at())


def build_nc():
    nc = bass.Bass("TRN2", target_bir_lowering=False)

    x_fm = nc.dram_tensor("x_fm", [HID, T], F32, kind="ExternalInput")
    xpre_fm = nc.dram_tensor("xpre_fm", [HID, T], F32, kind="ExternalInput")
    w_in = nc.dram_tensor("w_in", [HID, 5136], F32, kind="ExternalInput")
    w_out = nc.dram_tensor("w_out", [HID, HID], F32, kind="ExternalInput")
    w_up_t = nc.dram_tensor("w_up_t", [HID, 8192], F32, kind="ExternalInput")
    w_dn_t = nc.dram_tensor("w_dn_t", [8192, HID], F32, kind="ExternalInput")
    conv_w = nc.dram_tensor("conv_w", [128, 96], F32, kind="ExternalInput")
    conv_b = nc.dram_tensor("conv_b", [128, 24], F32, kind="ExternalInput")
    alog_d = nc.dram_tensor("alog_d", [NH, 1], F32, kind="ExternalInput")
    dtb_d = nc.dram_tensor("dtb_d", [NH, 1], F32, kind="ExternalInput")
    drep_d = nc.dram_tensor("drep_d", [128, NH], F32, kind="ExternalInput")
    nwst_d = nc.dram_tensor("nwst_d", [128, 16], F32, kind="ExternalInput")
    upsc_d = nc.dram_tensor("upsc_d", [128, 64], F32, kind="ExternalInput")
    dnsc_d = nc.dram_tensor("dnsc_d", [128, 16], F32, kind="ExternalInput")
    outsc_d = nc.dram_tensor("outsc_d", [128, 1], F32, kind="ExternalInput")
    s0sel_d = nc.dram_tensor("s0sel_d", [128, 1], F32, kind="ExternalInput")

    out_fm = nc.dram_tensor("out_fm", [HID, T], F32, kind="ExternalOutput")
    skind = "ExternalOutput" if DBG else "Internal"
    x1_scr = nc.dram_tensor("x1_scr", [HID, T], BF16, kind=skind)
    hid_scr = nc.dram_tensor("hid_scr", [8192, T], BF16, kind=skind)
    wdn_scr = nc.dram_tensor("wdn_scr", [HID, 8192], BF16, kind="Internal")
    wup_scr = nc.dram_tensor("wup_scr", [8192, HID], BF16, kind="Internal")
    if DBG:
        dbg_hbc = nc.dram_tensor("dbg_hbc", [128, 24 * HBW], BF16,
                                 kind="ExternalOutput")
        dbg_dt = nc.dram_tensor("dbg_dt", [16, T], F32, kind="ExternalOutput")
        dbg_s = nc.dram_tensor("dbg_s", [128, NH * 128], F32,
                               kind="ExternalOutput")
        dbg_g = nc.dram_tensor("dbg_g", [128, NH * Q], BF16,
                               kind="ExternalOutput")
        dbg_rdt = nc.dram_tensor("dbg_rdt", [16, T], F32,
                                 kind="ExternalOutput")
        dbg_dtt = nc.dram_tensor("dbg_dtt", [128, 128], F32,
                                 kind="ExternalOutput")
        dbg_pre = nc.dram_tensor("dbg_pre", [128, 8 * PREW], BF16,
                                 kind="ExternalOutput")
        dbg_s2 = nc.dram_tensor("dbg_s2", [128, NH * 128], F32,
                                kind="ExternalOutput")
        dbg_xh = nc.dram_tensor("dbg_xh", [128, 1024], BF16,
                                kind="ExternalOutput")
        dbg_bt = nc.dram_tensor("dbg_bt", [128, 1024], BF16,
                                kind="ExternalOutput")
        dbg_dld = nc.dram_tensor("dbg_dld", [16, 1024], F32,
                                 kind="ExternalOutput")

    ident_h = nc.inline_tensor(np.eye(128, dtype=np.float32), "ident")
    ones1_h = nc.inline_tensor(np.ones((1, 128), np.float32), "ones1")
    onesc_h = nc.inline_tensor(
        np.ones((128, 1), np.float32).astype(ml_dtypes.bfloat16), "onescol")
    oh_np = np.zeros((128, 16 * 16), np.float32)
    for h in range(16):
        oh_np[:, h * 16 + h] = 1.0
    oh_h = nc.inline_tensor(oh_np.astype(ml_dtypes.bfloat16), "oh16")
    sel_np = np.zeros((128, 16 * 128), np.float32)
    for h in range(16):
        sel_np[h, h * 128:(h + 1) * 128] = 1.0
    sel_h = nc.inline_tensor(sel_np.astype(ml_dtypes.bfloat16), "selbf")
    mk = np.zeros((128, 512), np.float32)
    for jc in range(2):
        for p_ in range(128):
            for i in range(256):
                if i < jc * 128 + p_:
                    mk[p_, jc * 256 + i] = -1e30
    mask_h = nc.inline_tensor(mk.astype(ml_dtypes.bfloat16), "maskadd")

    es = ExitStack()
    with es:
        sbuf = lambda n, sh, dt: es.enter_context(nc.sbuf_tensor(n, sh, dt))
        sch = Sched()
        ctr = {}
        for name, eng in [
            ("pe", TEN), ("ve", VEC), ("ac", SCA),
            ("c_const", GPS), ("c_xg", GPS), ("c_win0", GPS), ("c_win1", GPS),
            ("c_hw0", GPS), ("c_hw1", GPS), ("c_hw2", GPS), ("c_hw3", GPS),
            ("c_wup0", SYN), ("c_wup1", SYN), ("c_wdn0", SYN), ("c_wdn1", SYN),
            ("c_tr0", SYN), ("c_tr1", SYN), ("c_x1w0", SYN), ("c_x1w1", SYN),
            ("c_ldr", SYN), ("c_ow0", SYN), ("c_ow1", SYN),
            ("c_ldr2", GPS), ("bar", None),
        ]:
            sem = es.enter_context(nc.semaphore(name))
            ctr[name] = Ctr(sem, eng)
        pe, ve, ac = ctr["pe"], ctr["ve"], ctr["ac"]

        pb = [Bank(es.enter_context(nc.psum_tensor(f"pb{i}", [128, 512], F32)))
              for i in range(8)]

        identb = sbuf("identb", [128, 128], F32)
        identbf = sbuf("identbf", [128, 128], BF16)
        ones1b = sbuf("ones1b", [1, 128], F32)
        onescb = sbuf("onescb", [128, 1], BF16)
        ohb = sbuf("ohb", [128, 256], BF16)
        selb = sbuf("selb", [128, 2048], BF16)
        maskb = sbuf("maskb", [128, 512], BF16)
        convwb = sbuf("convwb", [128, 96], F32)
        convbb = sbuf("convbb", [128, 24], F32)
        drepb = sbuf("drepb", [128, 16], F32)
        nwstb = sbuf("nwstb", [128, 16], F32)
        upscb = sbuf("upscb", [128, 64], F32)
        dnscb = sbuf("dnscb", [128, 16], F32)
        outscb = sbuf("outscb", [128, 1], F32)
        s0selb = sbuf("s0selb", [128, 1], F32)
        alogb = sbuf("alogb", [16, 1], F32)
        dtbb = sbuf("dtbb", [16, 1], F32)
        avecb = sbuf("avecb", [16, 1], F32)
        dsob = sbuf("dsob", [128, 16], F32)
        dtfull = sbuf("dtfull", [16, T], F32)
        dach = sbuf("dach", [16, Q], F32)
        cumch = sbuf("cumch", [16, Q], F32)
        ecumch = sbuf("ecumch", [16, Q], F32)
        dldtch = sbuf("dldtch", [16, Q], F32)
        cdch = sbuf("cdch", [16, Q], F32)
        lastc = sbuf("lastc", [16, 1], F32)
        dte1 = sbuf("dte1", [16, 512], F32)
        dte2 = sbuf("dte2", [16, 512], F32)
        crowb = sbuf("crowb", [1, 512], BF16)
        ones16 = sbuf("ones16", [16, Q], F32)
        rtmp = sbuf("rtmp", [16, Q], F32)
        cumsp = sbuf("cumsp", [128, 3 * Q], BF16)
        ecusp = sbuf("ecusp", [128, 2 * Q], BF16)
        rssp = sbuf("rssp", [128, 2 * Q], BF16)
        gnrs = sbuf("gnrs", [16, Q], F32)
        dtT = sbuf("dtT", [128, 128], F32)
        S = sbuf("S", [128, NH * 128], F32)
        halo24 = sbuf("halo24", [128, 24 * 3], BF16)
        xpreh = sbuf("xpreh", [128, 16 * 3], BF16)
        sqt = sbuf("sqt", [128, 1024], BF16)
        epsb = sbuf("epsb", [128, 1], F32)
        scl128b = sbuf("scl128b", [128, 1], F32)

        cc = ctr["c_const"]
        for dst, src in [(identb, ident_h), (ones1b, ones1_h),
                         (onescb, onesc_h), (ohb, oh_h), (selb, sel_h),
                         (maskb, mask_h)]:
            sch.op(GPS, "dma_start", dst[:], src[:, :], incs=[cc.bump(16)])
        sch.op(GPS, "dma_start", convwb[:], conv_w[:, :],
               incs=[cc.bump(16)])
        sch.op(GPS, "dma_start", convbb[:], conv_b[:, :], incs=[cc.bump(16)])
        for dst, src in [(drepb, drep_d), (nwstb, nwst_d), (upscb, upsc_d),
                         (dnscb, dnsc_d), (outscb, outsc_d),
                         (s0selb, s0sel_d), (alogb, alog_d), (dtbb, dtb_d)]:
            sch.op(GPS, "dma_start", dst[:], src[:, :], incs=[cc.bump(16)])
        sch.op(SCA, "activation", avecb[:], alogb[:], AF.Exp,
               waits=[cc.at()], incs=[ac.bump(1)])
        sch.op(VEC, "tensor_scalar", avecb[:], avecb[:], -1.0, None, OP.mult,
               waits=[ac.at()], incs=[ve.bump(1)])
        sch.op(VEC, "tensor_scalar", dsob[:], dnscb[:], outscb[:, 0:1], None,
               OP.mult, waits=[cc.at()], incs=[ve.bump(1)])
        sch.op(VEC, "tensor_copy", identbf[:], identb[:], waits=[cc.at()],
               incs=[ve.bump(1)])
        sch.op(VEC, "memset", ones16[:], 1.0, incs=[ve.bump(1)])
        sch.op(VEC, "memset", epsb[:], EPS, incs=[ve.bump(1)])
        sch.op(VEC, "memset", scl128b[:], 1.0 / 128.0, incs=[ve.bump(1)])
        sch.op(VEC, "memset", S[:], 0.0, incs=[ve.bump(1)])
        sch.op(VEC, "memset", cumsp[:], 0.0, incs=[ve.bump(1)])
        sch.op(VEC, "memset", ecusp[:], 0.0, incs=[ve.bump(1)])
        sch.op(VEC, "memset", rssp[:], 0.0, incs=[ve.bump(1)])

        def split2(dst_sp, src, n=Q):
            sch.op(VEC, "tensor_copy", dst_sp[0:16, 0:n], src,
                   waits=[(pe, pe.val)], incs=[ve.bump(1)])
            sch.op(VEC, "tensor_tensor", rtmp[:, 0:n], src,
                   dst_sp[0:16, 0:n], OP.subtract, incs=[ve.bump(1)])
            sch.op(VEC, "tensor_copy", dst_sp[0:16, n:2 * n], rtmp[:, 0:n],
                   incs=[ve.bump(1)])

        def split3(dst_sp, src, n=Q):
            sch.op(VEC, "tensor_copy", dst_sp[0:16, 0:n], src,
                   waits=[(pe, pe.val)], incs=[ve.bump(1)])
            sch.op(VEC, "tensor_tensor", rtmp[:, 0:n], src,
                   dst_sp[0:16, 0:n], OP.subtract, incs=[ve.bump(1)])
            sch.op(VEC, "tensor_copy", dst_sp[0:16, n:2 * n], rtmp[:, 0:n],
                   incs=[ve.bump(1)])
            sch.op(VEC, "tensor_tensor", rtmp[:, 0:n], rtmp[:, 0:n],
                   dst_sp[0:16, n:2 * n], OP.subtract, incs=[ve.bump(1)])
            sch.op(VEC, "tensor_copy", dst_sp[0:16, 2 * n:3 * n],
                   rtmp[:, 0:n], incs=[ve.bump(1)])

        def dt_block_evac(bank, tslice, e1, e2):
            sch.op(SCA, "activation", e1, bank.t[0:16, 0:512], AF.Exp,
                   bias=dtbb[:, 0:1], waits=[pe.at()], incs=[ac.bump(1)])
            bank.release(ac)
            sch.op(VEC, "tensor_scalar", e2, e1, 1.0, None, OP.add,
                   waits=[ac.at()], incs=[ve.bump(1)])
            sch.op(SCA, "activation", dtfull[:, tslice], e2, AF.Ln,
                   waits=[ve.at()], incs=[ac.bump(1)])

        def dt_chunk(tok0, need_scores):
            dsl = dtfull[:, tok0:tok0 + Q]
            sch.op(VEC, "tensor_scalar", dach[:], dsl, avecb[:, 0:1], None,
                   OP.mult,
                   waits=[ac.at(), (pe, pe.val)]
                   + ([ctr["c_ow0"].at()] if DBG else []),
                   incs=[ve.bump(1)])
            sch.op(VEC, "tensor_tensor_scan", cumch[:], ones16[:], dach[:],
                   0.0, OP.mult, OP.add, incs=[ve.bump(1)])
            sch.op(SCA, "activation", ecumch[:], cumch[:], AF.Exp,
                   waits=[ve.at()], incs=[ac.bump(1)])
            sch.op(SCA, "copy", lastc[:], cumch[:, Q - 1:Q],
                   incs=[ac.bump(1)])
            sch.op(VEC, "tensor_scalar", dldtch[:], cumch[:],
                   lastc[:, 0:1], -1.0, OP.subtract, OP.mult,
                   waits=[ac.at()], incs=[ve.bump(1)])
            sch.op(SCA, "activation", dldtch[:], dldtch[:], AF.Exp,
                   waits=[ve.at()], incs=[ac.bump(1)])
            sch.op(VEC, "tensor_tensor", dldtch[:], dldtch[:], dsl, OP.mult,
                   waits=[ac.at()], incs=[ve.bump(1)])
            sch.op(VEC, "tensor_scalar", cdch[:], cumch[:], 0.0,
                   ecumch[:, Q - 1:Q], OP.mult, OP.add, waits=[ac.at()],
                   incs=[ve.bump(1)])
            if need_scores:
                split3(cumsp, cumch[:])
                split2(ecusp, ecumch[:])
            bk = pb[6]
            w0 = bk.take() + [ve.at(), ac.at(), cc.at(), (pe, pe.val)]
            first = True
            for q, src in enumerate([cumch, dldtch, None, cdch]):
                for jc in range(2):
                    inp = (dtfull[:, tok0 + jc * 128:tok0 + (jc + 1) * 128]
                           if src is None
                           else src[:, jc * 128:(jc + 1) * 128])
                    sch.op(TEN, "transpose",
                           bk.t[:, jc * 64 + q * 16:jc * 64 + (q + 1) * 16],
                           inp, identb[0:16, 0:16],
                           waits=(w0 if first else []),
                           incs=[pe.bump(1)])
                    first = False
            sch.op(SCA, "copy", dtT[:], bk.t[:, 0:128],
                   waits=[pe.at()], incs=[ac.bump(1)])
            bk.release(ac)

        def conv_channel(hbuf, stride, ch, seg0, ntok, tmp):
            base = ch * stride
            for k in range(4):
                src = hbuf[:, base + seg0 + k:base + seg0 + k + ntok]
                wcol = convwb[:, ch * 4 + k:ch * 4 + k + 1]
                if k == 0:
                    sch.op(VEC, "tensor_scalar", tmp[:, 0:ntok], src, wcol,
                           None, OP.mult, waits=[(ac, ac.val)],
                           incs=[ve.bump(1)])
                else:
                    sch.op(VEC, "scalar_tensor_tensor", tmp[:, 0:ntok], src,
                           wcol, tmp[:, 0:ntok], OP.mult, OP.add,
                           incs=[ve.bump(1)])
            sch.op(SCA, "activation",
                   hbuf[:, base + seg0 + 3:base + seg0 + 3 + ntok],
                   tmp[:, 0:ntok], AF.Silu, bias=convbb[:, ch:ch + 1],
                   waits=[ve.at()], incs=[ac.bump(1)])

        def transposes_chunk(hbuf, stride, c, trc):
            for g4 in range(4):
                for jc in range(2):
                    for dst, cho in ((xhT, g4), (BT, 4 + g4)):
                        off = cho * stride + 3 + c * Q + jc * 128
                        sch.op(SYN, "dma_start_transpose",
                               dst[:, (g4 * 2 + jc) * 128:
                                   (g4 * 2 + jc + 1) * 128],
                               hbuf[:, off:off + 128],
                               waits=[ac.at(), (pe, pe.val)],
                               incs=[trc.bump(16)])

        def sschunk_state(trc):
            schist = {}
            for h in range(16):
                g4 = h // 4
                slot = h % 2
                for jc in range(2):
                    sch.op(VEC, "tensor_scalar",
                           Bsb[:, slot * 256 + jc * 128:
                               slot * 256 + (jc + 1) * 128],
                           BT[:, (g4 * 2 + jc) * 128:(g4 * 2 + jc + 1) * 128],
                           dtT[:, jc * 64 + 16 + h:jc * 64 + 17 + h], None,
                           OP.mult,
                           waits=[trc.at(), (ac, ac.val)]
                           + ([schist[h - 2]] if h >= 2 else []),
                           incs=[ve.bump(1)])
                bk = pb[6 + (h // 4) % 2]
                qoff = (h % 4) * 128
                w0 = (bk.take() if h % 4 == 0 else [])
                for jc in range(2):
                    sch.op(TEN, "matmul", bk.t[:, qoff:qoff + 128],
                           Bsb[:, slot * 256 + jc * 128:
                               slot * 256 + (jc + 1) * 128],
                           xhT[:, (g4 * 2 + jc) * 128:(g4 * 2 + jc + 1) * 128],
                           start=(jc == 0), stop=(jc == 1),
                           waits=(w0 if jc == 0 else []) + [ve.at(), trc.at()],
                           incs=[pe.bump(1)] if jc == 1 else [])
                schist[h] = (pe, pe.val)
                if h % 4 == 3:
                    for hh in range(h - 3, h + 1):
                        sch.op(VEC, "scalar_tensor_tensor",
                               S[:, hh * 128:(hh + 1) * 128],
                               S[:, hh * 128:(hh + 1) * 128],
                               dtT[:, 48 + hh:49 + hh],
                               bk.t[:, (hh % 4) * 128:(hh % 4 + 1) * 128],
                               OP.mult, OP.add,
                               waits=[pe.at(), (ac, ac.val)],
                               incs=[ve.bump(1)])
                    bk.release(ve)

        def emit_ln(rb, dst_dram, tok_slice, ab, c_row, otmp, osl, owc,
                    ofree, out_f32):
            bk1, bk2 = pb[4], pb[5]
            w1 = bk1.take() + [ve.at()]
            w2 = bk2.take() + [ac.at()]
            pehist = {}
            for chn in range(16):
                sl = np.s_[:, chn * 512:(chn + 1) * 512]
                sq = sqt[:, (chn % 2) * 512:(chn % 2 + 1) * 512]
                sch.op(SCA, "activation", sq, rb[sl], AF.Square,
                       waits=[ve.at()] + ([pehist[chn - 2]] if chn >= 2 else []),
                       incs=[ac.bump(1)])
                sch.op(TEN, "matmul", bk1.t[0:1, 0:512], onescb[:, 0:1],
                       rb[sl], start=(chn == 0), stop=(chn == 15),
                       waits=(w1 if chn == 0 else []))
                sch.op(TEN, "matmul", bk2.t[0:1, 0:512], onescb[:, 0:1], sq,
                       start=(chn == 0), stop=(chn == 15),
                       waits=(w2 if chn == 0 else []) + [ac.at()],
                       incs=[pe.bump(1)])
                pehist[chn] = (pe, pe.val)
            A = ab[0:1, 0:512]
            B = ab[0:1, 512:1024]
            sch.op(VEC, "tensor_scalar", A, bk1.t[0:1, 0:512], 1.0 / 2048.0,
                   None, OP.mult, waits=[pe.at()], incs=[ve.bump(1)])
            bk1.release(ve)
            sch.op(VEC, "tensor_scalar", B, bk2.t[0:1, 0:512], 1.0 / 2048.0,
                   None, OP.mult, incs=[ve.bump(1)])
            bk2.release(ve)
            sch.op(VEC, "tensor_tensor", c_row, A, A, OP.mult,
                   incs=[ve.bump(1)])
            sch.op(VEC, "tensor_tensor", B, B, c_row, OP.subtract,
                   incs=[ve.bump(1)])
            sch.op(SCA, "activation", B, B, AF.Sqrt, bias=epsb[0:1, 0:1],
                   waits=[ve.at()], incs=[ac.bump(1)])
            sch.op(VEC, "reciprocal", B, B, waits=[ac.at()],
                   incs=[ve.bump(1)])
            sch.op(VEC, "tensor_tensor", A, A, B, OP.mult, incs=[ve.bump(1)])
            sch.op(VEC, "tensor_scalar", A, A, -1.0, None, OP.mult,
                   incs=[ve.bump(1)])
            bkr, bkn = pb[6], pb[7]
            sch.op(TEN, "matmul", bkr.t[:, 0:512], ones1b[0:1, :], B,
                   start=True, stop=True, waits=bkr.take() + [ve.at()],
                   incs=[pe.bump(1)])
            sch.op(TEN, "matmul", bkn.t[:, 0:512], ones1b[0:1, :], A,
                   start=True, stop=True, waits=bkn.take(),
                   incs=[pe.bump(1)])
            for chn in range(16):
                sl = np.s_[:, chn * 512:(chn + 1) * 512]
                oslot = chn % 2
                t1 = otmp[:, 0:512]
                ot = osl[oslot]
                wc = owc[oslot]
                sch.op(VEC, "tensor_tensor", t1, rb[sl], bkr.t[:, 0:512],
                       OP.mult, waits=[pe.at()], incs=[ve.bump(1)])
                sch.op(VEC, "tensor_tensor", ot, t1, bkn.t[:, 0:512], OP.add,
                       waits=[ofree[oslot]], incs=[ve.bump(1)])
                sch.op(SYN, "dma_start",
                       dst_dram[chn * 128:(chn + 1) * 128, tok_slice], ot,
                       waits=[ve.at()], incs=[wc.bump(16)])
                ofree[oslot] = wc.at()
            bkr.release(ve)
            bkn.release(ve)

        esA = ExitStack()
        sbufA = lambda n, sh, dt: esA.enter_context(nc.sbuf_tensor(n, sh, dt))
        xhT = sbufA("xhT", [128, 1024], BF16)
        BT = sbufA("BT", [128, 1024], BF16)
        Bsb = sbufA("Bsb", [128, 512], BF16)
        convtmp = sbufA("convtmp", [128, 1024], F32)
        wblk = sbufA("wblk", [128, 2 * 4096], BF16)
        wdtb = sbufA("wdtb", [128, 256], BF16)
        wfree = [None, None]
        winc = [ctr["c_win0"], ctr["c_win1"]]

        def load_wblk(slot, src_dram, c0, ncol=256):
            sch.op(GPS, "dma_start",
                   wblk[:, slot * 4096:slot * 4096 + 16 * ncol]
                   .rearrange("p (k c) -> p k c", k=16),
                   src_dram[:, c0:c0 + ncol]
                   .rearrange("(k p) c -> p k c", p=128),
                   waits=[wfree[slot]], incs=[winc[slot].bump(16)])

        def wtile(slot, kc, sub):
            base = slot * 4096 + kc * 256 + sub * 128
            return wblk[:, base:base + 128]

        esR = ExitStack()
        xpreb = esR.enter_context(nc.sbuf_tensor("xpreb", [128, 16 * T], BF16))
        hbcpre = esR.enter_context(
            nc.sbuf_tensor("hbcpre", [128, 8 * PREW], BF16))
        wsf = esR.enter_context(nc.sbuf_tensor("wsf", [128, 2 * 2048], F32))
        wsb = esR.enter_context(nc.sbuf_tensor("wsb", [128, 2 * 2048], BF16))
        wupw = ctr["c_x1w1"]
        wsldc = [ctr["c_wup0"], ctr["c_wup1"]]
        ws_state = {"i": 0, "ldfree": [None, None], "sgfree": [None, None]}

        def emit_wup_sign_unit():
            i = ws_state["i"]
            if i >= 64:
                return
            ws_state["i"] = i + 1
            slot = i % 2
            sch.op(GPS, "dma_start",
                   wsf[:, slot * 2048:(slot + 1) * 2048]
                   .rearrange("p (k c) -> p k c", k=16),
                   w_up_t[:, i * 128:(i + 1) * 128]
                   .rearrange("(k p) c -> p k c", p=128),
                   waits=[ws_state["ldfree"][slot]],
                   incs=[wsldc[slot].bump(16)])
            sch.op(SCA, "sign", wsb[:, slot * 2048:(slot + 1) * 2048],
                   wsf[:, slot * 2048:(slot + 1) * 2048],
                   waits=[wsldc[slot].at(), ws_state["sgfree"][slot]],
                   incs=[ac.bump(1)])
            ws_state["ldfree"][slot] = ac.at()
            sch.op(SYN, "dma_start", wup_scr[i * 128:(i + 1) * 128, :],
                   wsb[:, slot * 2048:(slot + 1) * 2048],
                   waits=[(ac, ac.val)], incs=[wupw.bump(16)])
            ws_state["sgfree"][slot] = wupw.at()
        cx = ctr["c_xg"]
        sch.op(GPS, "dma_start", xpreb.rearrange("p (k t) -> p k t", k=16),
               xpre_fm.rearrange("(k p) t -> p k t", p=128),
               incs=[cx.bump(16)])
        for kc in range(16):
            sch.op(VEC, "tensor_copy", xpreh[:, kc * 3:(kc + 1) * 3],
                   xpreb[:, kc * T + 2045:kc * T + 2048],
                   waits=[cx.at()], incs=[ve.bump(1)])

        for b in range(4):
            slot = b % 2
            load_wblk(slot, w_in, 2048 + b * 256)
            for sub in range(2):
                ch = b * 2 + sub
                for tt in range(4):
                    bk = pb[(b * 8 + sub * 4 + tt) % 4]
                    w0 = bk.take() + [winc[slot].at(), cx.at()]
                    for kc in range(16):
                        sch.op(TEN, "matmul", bk.t[:], wtile(slot, kc, sub),
                               xpreb[:, kc * T + tt * 512:
                                     kc * T + (tt + 1) * 512],
                               start=(kc == 0), stop=(kc == 15),
                               waits=(w0 if kc == 0 else []),
                               incs=[pe.bump(1)] if kc == 15 else [])
                    dst = hbcpre[:, ch * PREW + 3 + tt * 512:
                                 ch * PREW + 3 + (tt + 1) * 512]
                    sch.op(VEC, "tensor_copy", dst, bk.t[:], waits=[pe.at()],
                           incs=[ve.bump(1)])
                    bk.release(ve)
            wfree[slot] = pe.at()
            for _ in range(8):
                emit_wup_sign_unit()
        sch.op(GPS, "dma_start", wdtb.rearrange("p (k c) -> p k c", k=16),
               w_in[:, 5120:5136].rearrange("(k p) c -> p k c", p=128),
               waits=[wfree[0]], incs=[winc[0].bump(16)])
        for tt in range(4):
            bk = pb[tt % 4]
            w0 = bk.take() + [winc[0].at()]
            for kc in range(16):
                sch.op(TEN, "matmul", bk.t[0:16, 0:512],
                       wdtb[:, kc * 16:(kc + 1) * 16],
                       xpreb[:, kc * T + tt * 512:kc * T + (tt + 1) * 512],
                       start=(kc == 0), stop=(kc == 15),
                       waits=(w0 if kc == 0 else []),
                       incs=[pe.bump(1)] if kc == 15 else [])
            dt_block_evac(bk, np.s_[tt * 512:(tt + 1) * 512],
                          dte1[:], dte2[:])
        wfree[0] = pe.at()

        for ch in range(8):
            sch.op(VEC, "tensor_copy", halo24[:, ch * 3:ch * 3 + 3],
                   hbcpre[:, ch * PREW + 3 + 2045:ch * PREW + 3 + 2048],
                   waits=[ac.at()], incs=[ve.bump(1)])
        for ch in range(8):
            sch.op(VEC, "memset", hbcpre[:, ch * PREW:ch * PREW + 3], 0.0,
                   incs=[ve.bump(1)])
        for ch in range(8):
            conv_channel(hbcpre, PREW, ch, 1024, 1024, convtmp)
            conv_channel(hbcpre, PREW, ch, 0, 1024, convtmp)
        if DBG:
            sch.op(SYN, "dma_start", dbg_rdt[:, :], dtfull[:],
                   waits=[ac.at()], incs=[ctr["c_ow0"].bump(16)])
            sch.op(SYN, "dma_start", dbg_pre[:, :], hbcpre[:],
                   waits=[ac.at()], incs=[ctr["c_ow0"].bump(16)])
        for c in range(8):
            dt_chunk(c * Q, need_scores=False)
            if DBG and c == 0:
                sch.op(SYN, "dma_start", dbg_dtt[:, :], dtT[:],
                       waits=[ve.at()], incs=[ctr["c_ow0"].bump(16)])
                sch.op(SYN, "dma_start", dbg_dld[:, 0:256], dldtch[:],
                       waits=[ve.at()], incs=[ctr["c_ow0"].bump(16)])
                sch.op(SYN, "dma_start", dbg_dld[:, 256:512], cumch[:],
                       waits=[ve.at()], incs=[ctr["c_ow0"].bump(16)])
                sch.op(SYN, "dma_start", dbg_dld[:, 512:768], ecumch[:],
                       waits=[ve.at()], incs=[ctr["c_ow0"].bump(16)])
                sch.op(SYN, "dma_start", dbg_dld[:, 768:1024], dach[:],
                       waits=[ve.at()], incs=[ctr["c_ow0"].bump(16)])
            trc = ctr["c_tr0"] if c % 2 == 0 else ctr["c_tr1"]
            transposes_chunk(hbcpre, PREW, c, trc)
            sschunk_state(trc)
            for _ in range(4):
                emit_wup_sign_unit()
            if DBG and c == 0:
                sch.op(SYN, "dma_start", dbg_xh[:, :], xhT[:],
                       waits=[trc.at()], incs=[ctr["c_ow0"].bump(16)])
                sch.op(SYN, "dma_start", dbg_bt[:, :], BT[:],
                       waits=[trc.at()], incs=[ctr["c_ow0"].bump(16)])
                sch.op(SYN, "dma_start", dbg_s2[:, :], S[:],
                       waits=[ve.at()], incs=[ctr["c_ow0"].bump(16)])
        if DBG and False:
            sch.op(SYN, "dma_start", dbg_s2[:, :], S[:],
                   waits=[ve.at()], incs=[ctr["c_ow0"].bump(16)])
        sch.op(VEC, "tensor_scalar", S[:], S[:], s0selb[:, 0:1], None,
               OP.mult, waits=[ctr["c_ow0"].at()] if DBG else [],
               incs=[ve.bump(1)])
        if DBG:
            sch.op(SYN, "dma_start", dbg_s[:, :], S[:], waits=[ve.at()],
                   incs=[ctr["c_ow0"].bump(16)])
        while ws_state["i"] < 64:
            emit_wup_sign_unit()
        sch.op(SYN, "wait_ge", wupw.sem, wupw.val)
        esR.close()
        sch.barrier(ctr["bar"])

        esG = ExitStack()
        sbufG = lambda n, sh, dt: esG.enter_context(nc.sbuf_tensor(n, sh, dt))
        S_bf = sbufG("S_bf", [128, NH * 128], BF16)
        DIb = sbufG("DIb", [128, 16 * 128], BF16)
        sdf = sbufG("sdf", [128, 512], F32)
        sxp = sbufG("sxp", [128, 512], BF16)
        mpb = sbufG("mpb", [128, 2 * 512], BF16)
        ctb = sbufG("ctb", [128, 2 * 256], BF16)
        gpre = sbufG("gpre", [128, 16 * 256], BF16)
        g2q = sbufG("g2q", [128, 2 * 256], BF16)
        x1o = sbufG("x1o", [128, 2 * 512], BF16)
        hbc = sbufG("hbc", [128, 24 * HBW], BF16)
        x1wc = [ctr["c_x1w0"], ctr["c_x1w1"]]

        for h in range(16):
            sch.op(VEC, "tensor_scalar", DIb[:, h * 128:(h + 1) * 128],
                   identbf[:], drepb[:, h:h + 1], None, OP.mult,
                   incs=[ve.bump(1)])

        xg_free = None
        x1free = [None, None]
        f2 = {"i": 0, "ldfree": [None, None], "x1free": None,
              "hslotfree": [None, None],
              "bufs": None, "x1tt": -1}
        hwc = [ctr["c_hw0"], ctr["c_hw1"], ctr["c_hw2"], ctr["c_hw3"]]
        f2ld = [ctr["c_wup0"], ctr["c_wup1"]]
        f2x1 = ctr["c_ldr"]

        def emit_f2_unit(nmax=64 * 4):
            i = f2["i"]
            if i >= nmax:
                return False
            f2["i"] = i + 1
            tt, hcol = i // 64, i % 64
            x1f_t, wupr, hidt_t = f2["bufs"]
            if f2["x1tt"] != tt:
                f2["x1tt"] = tt
                sch.op(SYN, "dma_start",
                       x1f_t.rearrange("p (k t) -> p k t", k=16),
                       x1_scr[:, tt * 512:(tt + 1) * 512]
                       .rearrange("(k p) t -> p k t", p=128),
                       waits=[x1wc[0].at(), x1wc[1].at(), f2["x1free"]],
                       incs=[f2x1.bump(16)])
            slot = i % 2
            sch.op(SYN, "dma_start",
                   wupr[:, slot * 2048:(slot + 1) * 2048],
                   wup_scr[hcol * 128:(hcol + 1) * 128, :],
                   waits=[(wupw, 64 * 16), f2["ldfree"][slot]],
                   incs=[f2ld[slot].bump(16)])
            bk = pb[1]
            w0 = bk.take() + [f2ld[slot].at(), f2x1.at()]
            for kc in range(16):
                sch.op(TEN, "matmul", bk.t[:],
                       wupr[:, slot * 2048 + kc * 128:
                            slot * 2048 + (kc + 1) * 128],
                       x1f_t[:, kc * 512:(kc + 1) * 512],
                       start=(kc == 0), stop=(kc == 15),
                       waits=(w0 if kc == 0 else []),
                       incs=[pe.bump(1)] if kc == 15 else [])
            f2["ldfree"][slot] = (pe, pe.val)
            f2["x1free"] = (pe, pe.val)
            hs = i % 2
            ht = hidt_t[:, hs * 512:(hs + 1) * 512]
            sch.op(SCA, "activation", ht, bk.t[:], AF.Relu,
                   scale=upscb[:, hcol:hcol + 1],
                   waits=[pe.at(), f2["hslotfree"][hs], (ac, ac.val)],
                   incs=[ac.bump(1)])
            bk.release(ac)
            sch.op(GPS, "dma_start",
                   hid_scr[hcol * 128:(hcol + 1) * 128,
                           tt * 512:(tt + 1) * 512],
                   ht, waits=[ac.at()], incs=[hwc[tt].bump(16)])
            f2["hslotfree"][hs] = hwc[tt].at()
            return True

        for g in range(2):
            esGg = ExitStack()
            gateb = esGg.enter_context(
                nc.sbuf_tensor(f"gate{g}", [128, 16 * TG], BF16))
            esXW = ExitStack()
            xgb = esXW.enter_context(
                nc.sbuf_tensor(f"xg{g}", [128, 16 * TG], BF16))
            sch.op(GPS, "dma_start", xgb.rearrange("p (k t) -> p k t", k=16),
                   x_fm[:, g * TG:(g + 1) * TG]
                   .rearrange("(k p) t -> p k t", p=128),
                   waits=[xg_free], incs=[cx.bump(16)])

            col_blocks = ([("hbc", 2048 + b * 256, b) for b in range(12)]
                          + [("dt", 5120, 0)]
                          + [("gate", b * 256, b) for b in range(8)])
            bkrot = 0
            for kind, c0, bidx in col_blocks:
                if kind == "dt":
                    sch.op(GPS, "dma_start",
                           wdtb.rearrange("p (k c) -> p k c", k=16),
                           w_in[:, 5120:5136]
                           .rearrange("(k p) c -> p k c", p=128),
                           waits=[wfree[0]], incs=[winc[0].bump(16)])
                    for tt in range(2):
                        bk = pb[bkrot % 4]; bkrot += 1
                        w0 = bk.take() + [winc[0].at(), cx.at()]
                        for kc in range(16):
                            sch.op(TEN, "matmul", bk.t[0:16, 0:512],
                                   wdtb[:, kc * 16:(kc + 1) * 16],
                                   xgb[:, kc * TG + tt * 512:
                                       kc * TG + (tt + 1) * 512],
                                   start=(kc == 0), stop=(kc == 15),
                                   waits=(w0 if kc == 0 else []),
                                   incs=[pe.bump(1)] if kc == 15 else [])
                        dt_block_evac(bk, np.s_[tt * 512:(tt + 1) * 512],
                                  dte1[:], dte2[:])
                    wfree[0] = pe.at()
                    continue
                slot = bidx % 2
                load_wblk(slot, w_in if kind == "hbc" else w_in, c0)
                for sub in range(2):
                    for tt in range(2):
                        bk = pb[bkrot % 4]; bkrot += 1
                        w0 = bk.take() + [winc[slot].at(), cx.at()]
                        for kc in range(16):
                            sch.op(TEN, "matmul", bk.t[:],
                                   wtile(slot, kc, sub),
                                   xgb[:, kc * TG + tt * 512:
                                       kc * TG + (tt + 1) * 512],
                                   start=(kc == 0), stop=(kc == 15),
                                   waits=(w0 if kc == 0 else []),
                                   incs=[pe.bump(1)] if kc == 15 else [])
                        if kind == "hbc":
                            ch = bidx * 2 + sub
                            dst = hbc[:, ch * HBW + 3 + tt * 512:
                                      ch * HBW + 3 + (tt + 1) * 512]
                            sch.op(VEC, "tensor_copy", dst, bk.t[:],
                                   waits=[pe.at()], incs=[ve.bump(1)])
                            bk.release(ve)
                        else:
                            ch = bidx * 2 + sub
                            dst = gateb[:, ch * TG + tt * 512:
                                        ch * TG + (tt + 1) * 512]
                            sch.op(SCA, "activation", dst, bk.t[:], AF.Silu,
                                   waits=[pe.at()], incs=[ac.bump(1)])
                            bk.release(ac)
                    if g == 0 and kind == "hbc" and bidx >= 4:
                        ch = bidx * 2 + sub
                        bk = pb[4]
                        w0 = bk.take()
                        for kc in range(16):
                            sch.op(TEN, "matmul", bk.t[:, 0:3],
                                   wtile(slot, kc, sub),
                                   xpreh[:, kc * 3:(kc + 1) * 3],
                                   start=(kc == 0), stop=(kc == 15),
                                   waits=(w0 if kc == 0 else []),
                                   incs=[pe.bump(1)] if kc == 15 else [])
                        sch.op(VEC, "tensor_copy",
                               hbc[:, ch * HBW:ch * HBW + 3], bk.t[:, 0:3],
                               waits=[pe.at()], incs=[ve.bump(1)])
                        bk.release(ve)
                if kind == "hbc":
                    for sub in range(2):
                        ch = bidx * 2 + sub
                        if g == 1 or ch < 8:
                            sch.op(VEC, "tensor_copy",
                                   hbc[:, ch * HBW:ch * HBW + 3],
                                   halo24[:, ch * 3:ch * 3 + 3],
                                   incs=[ve.bump(1)])
                        if g == 0:
                            sch.op(VEC, "tensor_copy",
                                   halo24[:, ch * 3:ch * 3 + 3],
                                   hbc[:, ch * HBW + 3 + TG - 3:
                                       ch * HBW + 3 + TG],
                                   incs=[ve.bump(1)])
                        conv_channel(hbc, HBW, ch, 0, TG, convtmp)
                wfree[slot] = pe.at()
            esXW.close()
            xg_free = pe.at()
            esF2i = ExitStack()
            if g == 1:
                x1f_i = esF2i.enter_context(
                    nc.sbuf_tensor("x1fi", [128, 16 * 512], BF16))
                wupr_i = esF2i.enter_context(
                    nc.sbuf_tensor("wupri", [128, 2 * 2048], BF16))
                hidt_i = esF2i.enter_context(
                    nc.sbuf_tensor("hidti", [128, 2 * 512], BF16))
                f2["bufs"] = (x1f_i, wupr_i, hidt_i)
            if DBG and g == 0:
                sch.op(SYN, "dma_start", dbg_hbc[:, :], hbc[:],
                       waits=[ac.at()], incs=[ctr["c_ow0"].bump(16)])
                sch.op(SYN, "dma_start", dbg_dt[:, 0:TG], dtfull[:, 0:TG],
                       waits=[ac.at()], incs=[ctr["c_ow0"].bump(16)])

            for c in range(4):
                sch.op(VEC, "tensor_copy", S_bf[:], S[:],
                       waits=[(pe, pe.val)], incs=[ve.bump(1)])
                dt_chunk(c * Q, need_scores=True)
                trc = ctr["c_tr0"] if c % 2 == 0 else ctr["c_tr1"]
                transposes_chunk(hbc, HBW, c, trc)
                yhist = {}
                sc2hist = {}
                for h in range(16):
                    g4 = h // 4
                    slot = h % 2
                    sell = selb[:, h * 128:(h + 1) * 128]
                    bkbe = pb[4 + slot]
                    w0 = bkbe.take() + [ve.at()]
                    for i in range(3):
                        sch.op(TEN, "matmul", bkbe.t[:, 0:256], sell,
                               cumsp[:, i * Q:(i + 1) * Q],
                               start=(i == 0), stop=(i == 2),
                               waits=(w0 if i == 0 else []))
                    for i in range(2):
                        sch.op(TEN, "matmul", bkbe.t[:, 256:512], sell,
                               ecusp[:, i * Q:(i + 1) * Q],
                               start=(i == 0), stop=(i == 1),
                               incs=[pe.bump(1)] if i == 1 else [])
                    bks = pb[0]
                    w0 = bks.take()
                    for jc in range(2):
                        sch.op(TEN, "matmul",
                               bks.t[:, jc * 256:(jc + 1) * 256],
                               hbc[:, (4 + g4) * HBW + 3 + c * Q + jc * 128:
                                   (4 + g4) * HBW + 3 + c * Q + (jc + 1) * 128],
                               hbc[:, (8 + h) * HBW + 3 + c * Q:
                                   (8 + h) * HBW + 3 + (c + 1) * Q],
                               start=True, stop=True,
                               waits=(w0 if jc == 0 else []),
                               incs=[pe.bump(1)] if jc == 1 else [])
                    for jc in range(2):
                        sch.op(VEC, "scalar_tensor_tensor",
                               sdf[:, jc * 256:(jc + 1) * 256],
                               bkbe.t[:, 0:256],
                               dtT[:, jc * 64 + h:jc * 64 + h + 1],
                               maskb[:, jc * 256:(jc + 1) * 256],
                               OP.subtract, OP.add,
                               waits=[pe.at(), (ac, ac.val)],
                               incs=[ve.bump(1)])
                    sch.op(SCA, "activation", sxp[:], sdf[:], AF.Exp,
                           waits=[ve.at()], incs=[ac.bump(1)])
                    for jc in range(2):
                        sch.op(VEC, "scalar_tensor_tensor",
                               mpb[:, slot * 512 + jc * 256:
                                   slot * 512 + (jc + 1) * 256],
                               bks.t[:, jc * 256:(jc + 1) * 256],
                               dtT[:, jc * 64 + 32 + h:jc * 64 + 33 + h],
                               sxp[:, jc * 256:(jc + 1) * 256],
                               OP.mult, OP.mult,
                               waits=[ac.at()] + ([yhist[h - 2]]
                                                  if h >= 2 else []),
                               incs=[ve.bump(1)])
                    bks.release(ve)
                    sch.op(VEC, "tensor_tensor",
                           ctb[:, slot * 256:(slot + 1) * 256],
                           hbc[:, (8 + h) * HBW + 3 + c * Q:
                               (8 + h) * HBW + 3 + (c + 1) * Q],
                           bkbe.t[:, 256:512], OP.mult, incs=[ve.bump(1)])
                    bkbe.release(ve)
                    bky = pb[2 + slot]
                    w0 = bky.take()
                    for jc in range(2):
                        sch.op(TEN, "matmul", bky.t[:, 0:256],
                               xhT[:, (g4 * 2 + jc) * 128:
                                   (g4 * 2 + jc + 1) * 128],
                               mpb[:, slot * 512 + jc * 256:
                                   slot * 512 + (jc + 1) * 256],
                               start=(jc == 0), stop=False,
                               waits=(w0 if jc == 0 else []) + [ve.at(),
                                                                trc.at()])
                    sch.op(TEN, "matmul", bky.t[:, 0:256],
                           DIb[:, h * 128:(h + 1) * 128],
                           hbc[:, g4 * HBW + 3 + c * Q:
                               g4 * HBW + 3 + (c + 1) * Q],
                           start=False, stop=False)
                    sch.op(TEN, "matmul", bky.t[:, 0:256],
                           S_bf[:, h * 128:(h + 1) * 128],
                           ctb[:, slot * 256:(slot + 1) * 256],
                           start=False, stop=True, incs=[pe.bump(1)])
                    yhist[h] = (pe, pe.val)
                    sch.op(VEC, "tensor_tensor",
                           gpre[:, h * 256:(h + 1) * 256], bky.t[:, 0:256],
                           gateb[:, h * TG + c * Q:h * TG + (c + 1) * Q],
                           OP.mult, waits=[pe.at(), (ac, ac.val)],
                           incs=[ve.bump(1)])
                    bky.release(ve)
                    for jc in range(2):
                        sch.op(VEC, "tensor_scalar",
                               Bsb[:, slot * 256 + jc * 128:
                                   slot * 256 + (jc + 1) * 128],
                               BT[:, (g4 * 2 + jc) * 128:
                                   (g4 * 2 + jc + 1) * 128],
                               dtT[:, jc * 64 + 16 + h:jc * 64 + 17 + h],
                               None, OP.mult,
                               waits=[(ac, ac.val)]
                               + ([sc2hist[h - 2]] if h >= 2 else []),
                               incs=[ve.bump(1)])
                    bksc = pb[6 + (h // 4) % 2]
                    qoff = (h % 4) * 128
                    w0 = (bksc.take() if h % 4 == 0 else [])
                    for jc in range(2):
                        sch.op(TEN, "matmul", bksc.t[:, qoff:qoff + 128],
                               Bsb[:, slot * 256 + jc * 128:
                                   slot * 256 + (jc + 1) * 128],
                               xhT[:, (g4 * 2 + jc) * 128:
                                   (g4 * 2 + jc + 1) * 128],
                               start=(jc == 0), stop=(jc == 1),
                               waits=(w0 if jc == 0 else []) + [ve.at()],
                               incs=[pe.bump(1)] if jc == 1 else [])
                    sc2hist[h] = (pe, pe.val)
                    if h % 4 == 3:
                        for hh in range(h - 3, h + 1):
                            sch.op(VEC, "scalar_tensor_tensor",
                                   S[:, hh * 128:(hh + 1) * 128],
                                   S[:, hh * 128:(hh + 1) * 128],
                                   dtT[:, 48 + hh:49 + hh],
                                   bksc.t[:, (hh % 4) * 128:
                                          (hh % 4 + 1) * 128],
                                   OP.mult, OP.add,
                                   waits=[pe.at(), (ac, ac.val)],
                                   incs=[ve.bump(1)])
                        bksc.release(ve)
                    if g == 1 and f2["bufs"] is not None:
                        for _ in range(8):
                            emit_f2_unit(nmax=128)
                bkgn = pb[4]
                wgn = bkgn.take()
                gn_hist = {}
                for h in range(16):
                    sq = g2q[:, (h % 2) * 256:(h % 2 + 1) * 256]
                    sch.op(SCA, "activation", sq,
                           gpre[:, h * 256:(h + 1) * 256], AF.Square,
                           waits=[ve.at()] + ([gn_hist[h - 2]]
                                              if h >= 2 else []),
                           incs=[ac.bump(1)])
                    sch.op(TEN, "matmul", bkgn.t[0:16, 0:256],
                           ohb[:, h * 16:(h + 1) * 16], sq,
                           start=(h == 0), stop=(h == 15),
                           waits=(wgn if h == 0 else []) + [ac.at()],
                           incs=[pe.bump(1)])
                    gn_hist[h] = (pe, pe.val)
                sch.op(SCA, "activation", gnrs[:], bkgn.t[0:16, 0:256],
                       AF.Sqrt, bias=epsb[0:16, 0:1],
                       scale=scl128b[0:16, 0:1],
                       waits=[pe.at()], incs=[ac.bump(1)])
                bkgn.release(ac)
                sch.op(VEC, "reciprocal", gnrs[:], gnrs[:], waits=[ac.at()],
                       incs=[ve.bump(1)])
                split2(rssp, gnrs[:])
                for h in range(16):
                    bkr = pb[5 + (h % 2) * 2]
                    sell = selb[:, h * 128:(h + 1) * 128]
                    w0 = bkr.take() + [ve.at()]
                    for i in range(2):
                        sch.op(TEN, "matmul", bkr.t[:, 0:256], sell,
                               rssp[:, i * Q:(i + 1) * Q],
                               start=(i == 0), stop=(i == 1),
                               waits=(w0 if i == 0 else []),
                               incs=[pe.bump(1)] if i == 1 else [])
                    sch.op(VEC, "scalar_tensor_tensor",
                           hbc[:, (8 + h) * HBW + 3 + c * Q:
                               (8 + h) * HBW + 3 + (c + 1) * Q],
                           bkr.t[:, 0:256], nwstb[:, h:h + 1],
                           gpre[:, h * 256:(h + 1) * 256],
                           OP.mult, OP.mult, waits=[pe.at()],
                           incs=[ve.bump(1)])
                    bkr.release(ve)
                if DBG and g == 0 and c == 0:
                    sch.op(SYN, "dma_start", dbg_g[:, :], gpre[:],
                           waits=[ve.at()], incs=[ctr["c_ow0"].bump(16)])
            f2["bufs"] = None
            esF2i.close()
            esGg.close()

            esO = ExitStack()
            r1b = esO.enter_context(
                nc.sbuf_tensor(f"r1_{g}", [128, 16 * 512], BF16))
            xresb = esO.enter_context(
                nc.sbuf_tensor(f"xres_{g}", [128, 16 * 512], BF16))
            for tt in range(2):
                sch.op(GPS, "dma_start",
                       xresb.rearrange("p (k t) -> p k t", k=16),
                       x_fm[:, g * TG + tt * 512:g * TG + (tt + 1) * 512]
                       .rearrange("(k p) t -> p k t", p=128),
                       waits=[(pe, pe.val), (ve, ve.val)],
                       incs=[cx.bump(16)])
                for dblk in range(8):
                    slot = dblk % 2
                    load_wblk(slot, w_out, dblk * 256)
                    for sub in range(2):
                        dcol = dblk * 2 + sub
                        bk = pb[bkrot % 4]; bkrot += 1
                        w0 = bk.take() + [winc[slot].at(), ve.at(), cx.at()]
                        for kc in range(16):
                            sch.op(TEN, "matmul", bk.t[:],
                                   wtile(slot, kc, sub),
                                   hbc[:, (8 + kc) * HBW + 3 + tt * 512:
                                       (8 + kc) * HBW + 3 + (tt + 1) * 512],
                                   start=(kc == 0), stop=(kc == 15),
                                   waits=(w0 if kc == 0 else []),
                                   incs=[pe.bump(1)] if kc == 15 else [])
                        sch.op(VEC, "scalar_tensor_tensor",
                               r1b[:, dcol * 512:(dcol + 1) * 512],
                               bk.t[:], 1.0,
                               xresb[:, dcol * 512:(dcol + 1) * 512],
                               OP.mult, OP.add, waits=[pe.at(), cx.at()],
                               incs=[ve.bump(1)])
                        bk.release(ve)
                    wfree[slot] = pe.at()
                emit_ln(r1b, x1_scr,
                        np.s_[g * TG + tt * 512:g * TG + (tt + 1) * 512],
                        convtmp, crowb[:], convtmp,
                        [x1o[:, 0:512], x1o[:, 512:1024]], x1wc, x1free,
                        out_f32=False)
            esO.close()
        esG.close()
        esA.close()
        sch.op(SYN, "wait_ge", x1wc[0].sem, x1wc[0].val)
        sch.op(SYN, "wait_ge", x1wc[1].sem, x1wc[1].val)
        sch.barrier(ctr["bar"])

        esF2 = ExitStack()
        x1f_2 = esF2.enter_context(
            nc.sbuf_tensor("x1f2", [128, 16 * 512], BF16))
        wupr_2 = esF2.enter_context(
            nc.sbuf_tensor("wupr2", [128, 2 * 2048], BF16))
        hidt_2 = esF2.enter_context(
            nc.sbuf_tensor("hidt2", [128, 2 * 512], BF16))
        f2["bufs"] = (x1f_2, wupr_2, hidt_2)
        f2["x1tt"] = -1
        f2["hslotfree"] = [None, None]
        f2["ldfree"] = [None, None]
        f2["x1free"] = None
        while emit_f2_unit():
            pass
        esF2.close()
        for i in range(4):
            sch.op(GPS, "wait_ge", hwc[i].sem, hwc[i].val)
        sch.barrier(ctr["bar"])

        esF3 = ExitStack()
        hidf = esF3.enter_context(nc.sbuf_tensor("hidf", [128, 64 * 512], BF16))
        x1k = esF3.enter_context(nc.sbuf_tensor("x1k", [128, 16 * 512], BF16))
        wdnf = esF3.enter_context(nc.sbuf_tensor("wdnf", [128, 2 * 2048], F32))
        wdnb = esF3.enter_context(nc.sbuf_tensor("wdnb", [128, 2 * 8192], BF16))
        r2b = esF3.enter_context(nc.sbuf_tensor("r2b", [128, 16 * 512], BF16))
        outt = esF3.enter_context(nc.sbuf_tensor("outt", [128, 2 * 512], F32))
        f3tmp = esF3.enter_context(nc.sbuf_tensor("f3tmp", [128, 1024], F32))
        wdnc = [ctr["c_wdn0"], ctr["c_wdn1"]]
        cl2 = ctr["c_ldr2"]
        wdnw = ctr["c_x1w0"]
        dnfree = [None, None]
        owc = [ctr["c_ow0"], ctr["c_ow1"]]
        outfree = [None, None]
        dn_hist = {}
        wdnw_hist = {}
        for tt in range(4):
            sch.op(GPS, "dma_start", hidf.rearrange("p (k t) -> p k t", k=64),
                   hid_scr[:, tt * 512:(tt + 1) * 512]
                   .rearrange("(k p) t -> p k t", p=128),
                   waits=[hwc[0].at(), hwc[1].at(), hwc[2].at(), hwc[3].at(),
                          (pe, pe.val)],
                   incs=[cl2.bump(16)])
            sch.op(GPS, "dma_start", x1k.rearrange("p (k t) -> p k t", k=16),
                   x1_scr[:, tt * 512:(tt + 1) * 512]
                   .rearrange("(k p) t -> p k t", p=128),
                   waits=[(ve, ve.val)], incs=[cl2.bump(16)])
            for dcol in range(16):
                wslot = dcol % 2
                if tt == 0:
                    for half in range(4):
                        hs = half % 2
                        sch.op(SYN, "dma_start",
                               wdnf[:, hs * 2048:(hs + 1) * 2048]
                               .rearrange("p (k c) -> p k c", k=16),
                               w_dn_t[half * 2048:(half + 1) * 2048,
                                      dcol * 128:(dcol + 1) * 128]
                               .rearrange("(k p) c -> p k c", p=128),
                               waits=[dnfree[hs]], incs=[wdnc[hs].bump(16)])
                        sch.op(SCA, "sign",
                               wdnb[:, wslot * 8192 + half * 2048:
                                    wslot * 8192 + (half + 1) * 2048],
                               wdnf[:, hs * 2048:(hs + 1) * 2048],
                               waits=[wdnc[hs].at(),
                                      dn_hist.get((tt * 16 + dcol) - 2),
                                      wdnw_hist.get(dcol - 2)],
                               incs=[ac.bump(1)])
                        dnfree[hs] = ac.at()
                    sch.op(SYN, "dma_start",
                           wdn_scr[dcol * 128:(dcol + 1) * 128, :],
                           wdnb[:, wslot * 8192:(wslot + 1) * 8192],
                           waits=[(ac, ac.val)], incs=[wdnw.bump(16)])
                    wdnw_hist[dcol] = wdnw.at()
                else:
                    sch.op(SYN, "dma_start",
                           wdnb[:, wslot * 8192:(wslot + 1) * 8192],
                           wdn_scr[dcol * 128:(dcol + 1) * 128, :],
                           waits=[wdnw.at(),
                                  dn_hist.get((tt * 16 + dcol) - 2)],
                           incs=[wdnc[0].bump(16)])
                bk = pb[dcol % 4]
                w0 = bk.take() + ([ac.at()] if tt == 0
                                  else [wdnc[0].at()]) + [cl2.at()]
                for hk in range(64):
                    sch.op(TEN, "matmul", bk.t[:],
                           wdnb[:, wslot * 8192 + hk * 128:
                                wslot * 8192 + (hk + 1) * 128],
                           hidf[:, hk * 512:(hk + 1) * 512],
                           start=(hk == 0), stop=(hk == 63),
                           waits=(w0 if hk == 0 else []),
                           incs=[pe.bump(1)] if hk == 63 else [])
                dn_hist[tt * 16 + dcol] = (pe, pe.val)
                sch.op(VEC, "scalar_tensor_tensor",
                       r2b[:, dcol * 512:(dcol + 1) * 512], bk.t[:],
                       dsob[:, dcol:dcol + 1],
                       x1k[:, dcol * 512:(dcol + 1) * 512],
                       OP.mult, OP.add, waits=[pe.at(), cl2.at()],
                       incs=[ve.bump(1)])
                bk.release(ve)
            emit_ln(r2b, out_fm, np.s_[tt * 512:(tt + 1) * 512],
                    f3tmp, crowb[:], f3tmp,
                    [outt[:, 0:512], outt[:, 512:1024]], owc, outfree,
                    out_f32=True)
        esF3.close()
        sch.op(SYN, "wait_ge", owc[0].sem, owc[0].val)
        sch.op(SYN, "wait_ge", owc[1].sem, owc[1].val)

        with nc.Block() as block:
            @block.tensor
            def _(e):
                sch.replay(TEN, e)

            @block.vector
            def _(e):
                sch.replay(VEC, e)

            @block.scalar
            def _(e):
                sch.replay(SCA, e)

            @block.gpsimd
            def _(e):
                sch.replay(GPS, e)

            @block.sync
            def _(e):
                sch.replay(SYN, e)

    return nc


_NC_CACHE = {}


def kernel(L0_attn_hidden_states, L0_attn_W_in, L0_attn_conv_weight,
           L0_attn_conv_bias, L0_attn_A_log, L0_attn_D, L0_attn_dt_bias,
           L0_attn_norm_weight, L0_attn_W_out, L0_ffn_up_weight,
           L0_ffn_down_weight, L0_ffn_up_scale, L0_ffn_down_scale,
           L0_ffn_output_scale):
    hs = np.asarray(L0_attn_hidden_states, np.float32)
    B, L, _ = hs.shape
    f32c = lambda a: np.ascontiguousarray(np.asarray(a, np.float32))

    shared = {
        "w_in": f32c(L0_attn_W_in),
        "w_out": f32c(L0_attn_W_out),
        "w_up_t": f32c(np.asarray(L0_ffn_up_weight).T),
        "w_dn_t": f32c(np.asarray(L0_ffn_down_weight).T),
        "conv_w": f32c(np.asarray(L0_attn_conv_weight)
                       .reshape(24, 128, 4).transpose(1, 0, 2)
                       .reshape(128, 96)),
        "conv_b": f32c(np.asarray(L0_attn_conv_bias).reshape(24, 128).T),
        "alog_d": f32c(np.asarray(L0_attn_A_log).reshape(NH, 1)),
        "dtb_d": f32c(np.asarray(L0_attn_dt_bias).reshape(NH, 1)),
        "drep_d": f32c(np.broadcast_to(np.asarray(L0_attn_D).reshape(1, NH),
                                       (128, NH))),
        "nwst_d": f32c(np.asarray(L0_attn_norm_weight).reshape(16, 128).T),
        "upsc_d": f32c(np.asarray(L0_ffn_up_scale).reshape(64, 128).T),
        "dnsc_d": f32c(np.asarray(L0_ffn_down_scale).reshape(16, 128).T),
        "outsc_d": f32c(np.broadcast_to(
            np.asarray(L0_ffn_output_scale).reshape(1, 1), (128, 1))),
    }

    in_maps = []
    for core in range(8):
        b, half = core // 2, core % 2
        m = dict(shared)
        m["x_fm"] = f32c(hs[b, half * T:(half + 1) * T, :].T)
        m["xpre_fm"] = (f32c(hs[b, 0:T, :].T) if half == 1
                        else np.zeros((HID, T), np.float32))
        m["s0sel_d"] = np.full((128, 1), float(half), np.float32)
        in_maps.append(m)

    if "v1" not in _NC_CACHE:
        _NC_CACHE["v1"] = build_nc()
    nc = _NC_CACHE["v1"]

    res = run_bass_kernel_spmd(nc, in_maps, core_ids=list(range(8)))

    out = np.empty((B, L, HID), np.float32)
    for core in range(8):
        b, half = core // 2, core % 2
        out[b, half * T:(half + 1) * T, :] = res.results[core]["out_fm"].T
    if DBG:
        kernel._dbg = res.results
    return out


# version 12
# speedup vs baseline: 1.0322x; 1.0026x over previous
import numpy as np
import ml_dtypes
from contextlib import ExitStack

import concourse.bass as bass
import concourse.mybir as mybir
from concourse.bass_utils import run_bass_kernel_spmd
from concourse.alu_op_type import AluOpType as OP

F32 = mybir.dt.float32
BF16 = mybir.dt.bfloat16
AF = mybir.ActivationFunctionType

TEN, VEC, SCA, GPS, SYN = "tensor", "vector", "scalar", "gpsimd", "sync"
ENGS = (TEN, VEC, SCA, GPS, SYN)

HID = 2048
NH = 16
Q = 256
EPS = 1e-5
T = 2048
TG = 1024
HBW = TG + 3
PREW = T + 3

DBG = False


class Ctr:
    def __init__(self, sem, owner=None):
        self.sem = sem
        self.val = 0
        self.owner = owner

    def bump(self, n):
        self.val += n
        return (self, n)

    def at(self, val=None):
        return (self, self.val if val is None else val)


class Sched:
    def __init__(self):
        self.q = {e: [] for e in ENGS}
        self.waited = {e: {} for e in ENGS}

    def op(self, eng, method, *args, waits=(), incs=(), **kwargs):
        ws = []
        for w in waits:
            if w is None:
                continue
            c, val = w
            if c.owner == eng or val <= 0:
                continue
            key = c.sem.num
            if self.waited[eng].get(key, -1) >= val:
                continue
            self.waited[eng][key] = val
            ws.append((c.sem, val))
        ics = []
        for c, n in incs:
            ics.append((c.sem, n))
        self.q[eng].append((method, args, kwargs, ws, ics))

    def barrier(self, bctr):
        target = bctr.val + 5
        for e in ENGS:
            self.q[e].append(("sem_inc", (bctr.sem, 1), {}, [], []))
        bctr.val = target
        for e in ENGS:
            self.q[e].append(("wait_ge", (bctr.sem, target), {}, [], []))
            self.waited[e][bctr.sem.num] = target

    def replay(self, eng_name, eng):
        for method, args, kwargs, ws, ics in self.q[eng_name]:
            for sem, val in ws:
                eng.wait_ge(sem, val)
            inst = getattr(eng, method)(*args, **kwargs)
            for sem, n in ics:
                inst.then_inc(sem, n, skip_validation=True)


class Bank:
    def __init__(self, t):
        self.t = t
        self.pending = []

    def take(self):
        ws = self.pending
        self.pending = []
        return ws

    def release(self, ctr):
        self.pending.append(ctr.at())


def build_nc():
    nc = bass.Bass("TRN2", target_bir_lowering=False)

    x_fm = nc.dram_tensor("x_fm", [HID, T], F32, kind="ExternalInput")
    xpre_fm = nc.dram_tensor("xpre_fm", [HID, T], F32, kind="ExternalInput")
    w_in = nc.dram_tensor("w_in", [HID, 5136], F32, kind="ExternalInput")
    w_out = nc.dram_tensor("w_out", [HID, HID], F32, kind="ExternalInput")
    w_up_t = nc.dram_tensor("w_up_t", [HID, 8192], F32, kind="ExternalInput")
    w_dn_t = nc.dram_tensor("w_dn_t", [8192, HID], F32, kind="ExternalInput")
    conv_w = nc.dram_tensor("conv_w", [128, 96], F32, kind="ExternalInput")
    conv_b = nc.dram_tensor("conv_b", [128, 24], F32, kind="ExternalInput")
    alog_d = nc.dram_tensor("alog_d", [NH, 1], F32, kind="ExternalInput")
    dtb_d = nc.dram_tensor("dtb_d", [NH, 1], F32, kind="ExternalInput")
    drep_d = nc.dram_tensor("drep_d", [128, NH], F32, kind="ExternalInput")
    nwst_d = nc.dram_tensor("nwst_d", [128, 16], F32, kind="ExternalInput")
    upsc_d = nc.dram_tensor("upsc_d", [128, 64], F32, kind="ExternalInput")
    dnsc_d = nc.dram_tensor("dnsc_d", [128, 16], F32, kind="ExternalInput")
    outsc_d = nc.dram_tensor("outsc_d", [128, 1], F32, kind="ExternalInput")
    s0sel_d = nc.dram_tensor("s0sel_d", [128, 1], F32, kind="ExternalInput")

    out_fm = nc.dram_tensor("out_fm", [HID, T], F32, kind="ExternalOutput")
    skind = "ExternalOutput" if DBG else "Internal"
    x1_scr = nc.dram_tensor("x1_scr", [HID, T], BF16, kind=skind)
    hid_scr = nc.dram_tensor("hid_scr", [8192, T], BF16, kind=skind)
    wdn_scr = nc.dram_tensor("wdn_scr", [HID, 8192], BF16, kind="Internal")
    wup_scr = nc.dram_tensor("wup_scr", [8192, HID], BF16, kind="Internal")
    if DBG:
        dbg_hbc = nc.dram_tensor("dbg_hbc", [128, 24 * HBW], BF16,
                                 kind="ExternalOutput")
        dbg_dt = nc.dram_tensor("dbg_dt", [16, T], F32, kind="ExternalOutput")
        dbg_s = nc.dram_tensor("dbg_s", [128, NH * 128], F32,
                               kind="ExternalOutput")
        dbg_g = nc.dram_tensor("dbg_g", [128, NH * Q], BF16,
                               kind="ExternalOutput")
        dbg_rdt = nc.dram_tensor("dbg_rdt", [16, T], F32,
                                 kind="ExternalOutput")
        dbg_dtt = nc.dram_tensor("dbg_dtt", [128, 128], F32,
                                 kind="ExternalOutput")
        dbg_pre = nc.dram_tensor("dbg_pre", [128, 8 * PREW], BF16,
                                 kind="ExternalOutput")
        dbg_s2 = nc.dram_tensor("dbg_s2", [128, NH * 128], F32,
                                kind="ExternalOutput")
        dbg_xh = nc.dram_tensor("dbg_xh", [128, 1024], BF16,
                                kind="ExternalOutput")
        dbg_bt = nc.dram_tensor("dbg_bt", [128, 1024], BF16,
                                kind="ExternalOutput")
        dbg_dld = nc.dram_tensor("dbg_dld", [16, 1024], F32,
                                 kind="ExternalOutput")

    ident_h = nc.inline_tensor(np.eye(128, dtype=np.float32), "ident")
    ones1_h = nc.inline_tensor(np.ones((1, 128), np.float32), "ones1")
    onesc_h = nc.inline_tensor(
        np.ones((128, 1), np.float32).astype(ml_dtypes.bfloat16), "onescol")
    oh_np = np.zeros((128, 16 * 16), np.float32)
    for h in range(16):
        oh_np[:, h * 16 + h] = 1.0
    oh_h = nc.inline_tensor(oh_np.astype(ml_dtypes.bfloat16), "oh16")
    sel_np = np.zeros((128, 16 * 128), np.float32)
    for h in range(16):
        sel_np[h, h * 128:(h + 1) * 128] = 1.0
    sel_h = nc.inline_tensor(sel_np.astype(ml_dtypes.bfloat16), "selbf")
    mk = np.zeros((128, 512), np.float32)
    for jc in range(2):
        for p_ in range(128):
            for i in range(256):
                if i < jc * 128 + p_:
                    mk[p_, jc * 256 + i] = -1e30
    mask_h = nc.inline_tensor(mk.astype(ml_dtypes.bfloat16), "maskadd")

    es = ExitStack()
    with es:
        sbuf = lambda n, sh, dt: es.enter_context(nc.sbuf_tensor(n, sh, dt))
        sch = Sched()
        ctr = {}
        for name, eng in [
            ("pe", TEN), ("ve", VEC), ("ac", SCA),
            ("c_const", GPS), ("c_xg", GPS), ("c_win0", GPS), ("c_win1", GPS),
            ("c_hw0", GPS), ("c_hw1", GPS), ("c_hw2", GPS), ("c_hw3", GPS),
            ("c_wup0", SYN), ("c_wup1", SYN), ("c_wdn0", SYN), ("c_wdn1", SYN),
            ("c_tr0", SYN), ("c_tr1", SYN), ("c_x1w0", SYN), ("c_x1w1", SYN),
            ("c_ldr", SYN), ("c_ow0", SYN), ("c_ow1", SYN),
            ("c_ldr2", GPS), ("bar", None),
        ]:
            sem = es.enter_context(nc.semaphore(name))
            ctr[name] = Ctr(sem, eng)
        pe, ve, ac = ctr["pe"], ctr["ve"], ctr["ac"]

        pb = [Bank(es.enter_context(nc.psum_tensor(f"pb{i}", [128, 512], F32)))
              for i in range(8)]

        identb = sbuf("identb", [128, 128], F32)
        identbf = sbuf("identbf", [128, 128], BF16)
        ones1b = sbuf("ones1b", [1, 128], F32)
        onescb = sbuf("onescb", [128, 1], BF16)
        ohb = sbuf("ohb", [128, 256], BF16)
        selb = sbuf("selb", [128, 2048], BF16)
        maskb = sbuf("maskb", [128, 512], BF16)
        convwb = sbuf("convwb", [128, 96], F32)
        convbb = sbuf("convbb", [128, 24], F32)
        drepb = sbuf("drepb", [128, 16], F32)
        nwstb = sbuf("nwstb", [128, 16], F32)
        upscb = sbuf("upscb", [128, 64], F32)
        dnscb = sbuf("dnscb", [128, 16], F32)
        outscb = sbuf("outscb", [128, 1], F32)
        s0selb = sbuf("s0selb", [128, 1], F32)
        alogb = sbuf("alogb", [16, 1], F32)
        dtbb = sbuf("dtbb", [16, 1], F32)
        avecb = sbuf("avecb", [16, 1], F32)
        dsob = sbuf("dsob", [128, 16], F32)
        dtfull = sbuf("dtfull", [16, T], F32)
        dach = sbuf("dach", [16, Q], F32)
        cumch = sbuf("cumch", [16, Q], F32)
        ecumch = sbuf("ecumch", [16, Q], F32)
        dldtch = sbuf("dldtch", [16, Q], F32)
        cdch = sbuf("cdch", [16, Q], F32)
        lastc = sbuf("lastc", [16, 1], F32)
        dte1 = sbuf("dte1", [16, 512], F32)
        dte2 = sbuf("dte2", [16, 512], F32)
        crowb = sbuf("crowb", [1, 512], BF16)
        ones16 = sbuf("ones16", [16, Q], F32)
        rtmp = sbuf("rtmp", [16, Q], F32)
        cumsp = sbuf("cumsp", [128, 3 * Q], BF16)
        ecusp = sbuf("ecusp", [128, 2 * Q], BF16)
        rssp = sbuf("rssp", [128, 2 * Q], BF16)
        gnrs = sbuf("gnrs", [16, Q], F32)
        dtT = sbuf("dtT", [128, 128], F32)
        S = sbuf("S", [128, NH * 128], F32)
        halo24 = sbuf("halo24", [128, 24 * 3], BF16)
        xpreh = sbuf("xpreh", [128, 16 * 3], BF16)
        sqt = sbuf("sqt", [128, 1024], BF16)
        epsb = sbuf("epsb", [128, 1], F32)
        scl128b = sbuf("scl128b", [128, 1], F32)

        cc = ctr["c_const"]
        for dst, src in [(identb, ident_h), (ones1b, ones1_h),
                         (onescb, onesc_h), (ohb, oh_h), (selb, sel_h),
                         (maskb, mask_h)]:
            sch.op(GPS, "dma_start", dst[:], src[:, :], incs=[cc.bump(16)])
        sch.op(GPS, "dma_start", convwb[:], conv_w[:, :],
               incs=[cc.bump(16)])
        sch.op(GPS, "dma_start", convbb[:], conv_b[:, :], incs=[cc.bump(16)])
        for dst, src in [(drepb, drep_d), (nwstb, nwst_d), (upscb, upsc_d),
                         (dnscb, dnsc_d), (outscb, outsc_d),
                         (s0selb, s0sel_d), (alogb, alog_d), (dtbb, dtb_d)]:
            sch.op(GPS, "dma_start", dst[:], src[:, :], incs=[cc.bump(16)])
        sch.op(SCA, "activation", avecb[:], alogb[:], AF.Exp,
               waits=[cc.at()], incs=[ac.bump(1)])
        sch.op(VEC, "tensor_scalar", avecb[:], avecb[:], -1.0, None, OP.mult,
               waits=[ac.at()], incs=[ve.bump(1)])
        sch.op(VEC, "tensor_scalar", dsob[:], dnscb[:], outscb[:, 0:1], None,
               OP.mult, waits=[cc.at()], incs=[ve.bump(1)])
        sch.op(VEC, "tensor_copy", identbf[:], identb[:], waits=[cc.at()],
               incs=[ve.bump(1)])
        sch.op(VEC, "memset", ones16[:], 1.0, incs=[ve.bump(1)])
        sch.op(VEC, "memset", epsb[:], EPS, incs=[ve.bump(1)])
        sch.op(VEC, "memset", scl128b[:], 1.0 / 128.0, incs=[ve.bump(1)])
        sch.op(VEC, "memset", S[:], 0.0, incs=[ve.bump(1)])
        sch.op(VEC, "memset", cumsp[:], 0.0, incs=[ve.bump(1)])
        sch.op(VEC, "memset", ecusp[:], 0.0, incs=[ve.bump(1)])
        sch.op(VEC, "memset", rssp[:], 0.0, incs=[ve.bump(1)])

        def split2(dst_sp, src, n=Q):
            sch.op(VEC, "tensor_copy", dst_sp[0:16, 0:n], src,
                   waits=[(pe, pe.val)], incs=[ve.bump(1)])
            sch.op(VEC, "tensor_tensor", rtmp[:, 0:n], src,
                   dst_sp[0:16, 0:n], OP.subtract, incs=[ve.bump(1)])
            sch.op(VEC, "tensor_copy", dst_sp[0:16, n:2 * n], rtmp[:, 0:n],
                   incs=[ve.bump(1)])

        def split3(dst_sp, src, n=Q):
            sch.op(VEC, "tensor_copy", dst_sp[0:16, 0:n], src,
                   waits=[(pe, pe.val)], incs=[ve.bump(1)])
            sch.op(VEC, "tensor_tensor", rtmp[:, 0:n], src,
                   dst_sp[0:16, 0:n], OP.subtract, incs=[ve.bump(1)])
            sch.op(VEC, "tensor_copy", dst_sp[0:16, n:2 * n], rtmp[:, 0:n],
                   incs=[ve.bump(1)])
            sch.op(VEC, "tensor_tensor", rtmp[:, 0:n], rtmp[:, 0:n],
                   dst_sp[0:16, n:2 * n], OP.subtract, incs=[ve.bump(1)])
            sch.op(VEC, "tensor_copy", dst_sp[0:16, 2 * n:3 * n],
                   rtmp[:, 0:n], incs=[ve.bump(1)])

        def dt_block_evac(bank, tslice, e1, e2):
            sch.op(SCA, "activation", e1, bank.t[0:16, 0:512], AF.Exp,
                   bias=dtbb[:, 0:1], waits=[pe.at()], incs=[ac.bump(1)])
            bank.release(ac)
            sch.op(VEC, "tensor_scalar", e2, e1, 1.0, None, OP.add,
                   waits=[ac.at()], incs=[ve.bump(1)])
            sch.op(SCA, "activation", dtfull[:, tslice], e2, AF.Ln,
                   waits=[ve.at()], incs=[ac.bump(1)])

        def dt_chunk(tok0, need_scores):
            dsl = dtfull[:, tok0:tok0 + Q]
            sch.op(VEC, "tensor_scalar", dach[:], dsl, avecb[:, 0:1], None,
                   OP.mult,
                   waits=[ac.at(), (pe, pe.val)]
                   + ([ctr["c_ow0"].at()] if DBG else []),
                   incs=[ve.bump(1)])
            sch.op(VEC, "tensor_tensor_scan", cumch[:], ones16[:], dach[:],
                   0.0, OP.mult, OP.add, incs=[ve.bump(1)])
            sch.op(SCA, "activation", ecumch[:], cumch[:], AF.Exp,
                   waits=[ve.at()], incs=[ac.bump(1)])
            sch.op(SCA, "copy", lastc[:], cumch[:, Q - 1:Q],
                   incs=[ac.bump(1)])
            sch.op(VEC, "tensor_scalar", dldtch[:], cumch[:],
                   lastc[:, 0:1], -1.0, OP.subtract, OP.mult,
                   waits=[ac.at()], incs=[ve.bump(1)])
            sch.op(SCA, "activation", dldtch[:], dldtch[:], AF.Exp,
                   waits=[ve.at()], incs=[ac.bump(1)])
            sch.op(VEC, "tensor_tensor", dldtch[:], dldtch[:], dsl, OP.mult,
                   waits=[ac.at()], incs=[ve.bump(1)])
            sch.op(VEC, "tensor_scalar", cdch[:], cumch[:], 0.0,
                   ecumch[:, Q - 1:Q], OP.mult, OP.add, waits=[ac.at()],
                   incs=[ve.bump(1)])
            if need_scores:
                split3(cumsp, cumch[:])
                split2(ecusp, ecumch[:])
            bk = pb[6]
            w0 = bk.take() + [ve.at(), ac.at(), cc.at(), (pe, pe.val)]
            first = True
            for q, src in enumerate([cumch, dldtch, None, cdch]):
                for jc in range(2):
                    inp = (dtfull[:, tok0 + jc * 128:tok0 + (jc + 1) * 128]
                           if src is None
                           else src[:, jc * 128:(jc + 1) * 128])
                    sch.op(TEN, "transpose",
                           bk.t[:, jc * 64 + q * 16:jc * 64 + (q + 1) * 16],
                           inp, identb[0:16, 0:16],
                           waits=(w0 if first else []),
                           incs=[pe.bump(1)])
                    first = False
            sch.op(SCA, "copy", dtT[:], bk.t[:, 0:128],
                   waits=[pe.at()], incs=[ac.bump(1)])
            bk.release(ac)

        def conv_channel(hbuf, stride, ch, seg0, ntok, tmp):
            base = ch * stride
            for k in range(4):
                src = hbuf[:, base + seg0 + k:base + seg0 + k + ntok]
                wcol = convwb[:, ch * 4 + k:ch * 4 + k + 1]
                if k == 0:
                    sch.op(VEC, "tensor_scalar", tmp[:, 0:ntok], src, wcol,
                           None, OP.mult, waits=[(ac, ac.val)],
                           incs=[ve.bump(1)])
                else:
                    sch.op(VEC, "scalar_tensor_tensor", tmp[:, 0:ntok], src,
                           wcol, tmp[:, 0:ntok], OP.mult, OP.add,
                           incs=[ve.bump(1)])
            sch.op(SCA, "activation",
                   hbuf[:, base + seg0 + 3:base + seg0 + 3 + ntok],
                   tmp[:, 0:ntok], AF.Silu, bias=convbb[:, ch:ch + 1],
                   waits=[ve.at()], incs=[ac.bump(1)])

        def transposes_chunk(hbuf, stride, c, trc):
            for g4 in range(4):
                for jc in range(2):
                    for dst, cho in ((xhT, g4), (BT, 4 + g4)):
                        off = cho * stride + 3 + c * Q + jc * 128
                        sch.op(SYN, "dma_start_transpose",
                               dst[:, (g4 * 2 + jc) * 128:
                                   (g4 * 2 + jc + 1) * 128],
                               hbuf[:, off:off + 128],
                               waits=[ac.at(), (pe, pe.val)],
                               incs=[trc.bump(16)])

        def sschunk_state(trc):
            schist = {}
            for h in range(16):
                g4 = h // 4
                slot = h % 2
                for jc in range(2):
                    sch.op(VEC, "tensor_scalar",
                           Bsb[:, slot * 256 + jc * 128:
                               slot * 256 + (jc + 1) * 128],
                           BT[:, (g4 * 2 + jc) * 128:(g4 * 2 + jc + 1) * 128],
                           dtT[:, jc * 64 + 16 + h:jc * 64 + 17 + h], None,
                           OP.mult,
                           waits=[trc.at(), (ac, ac.val)]
                           + ([schist[h - 2]] if h >= 2 else []),
                           incs=[ve.bump(1)])
                bk = pb[6 + (h // 4) % 2]
                qoff = (h % 4) * 128
                w0 = (bk.take() if h % 4 == 0 else [])
                for jc in range(2):
                    sch.op(TEN, "matmul", bk.t[:, qoff:qoff + 128],
                           Bsb[:, slot * 256 + jc * 128:
                               slot * 256 + (jc + 1) * 128],
                           xhT[:, (g4 * 2 + jc) * 128:(g4 * 2 + jc + 1) * 128],
                           start=(jc == 0), stop=(jc == 1),
                           waits=(w0 if jc == 0 else []) + [ve.at(), trc.at()],
                           incs=[pe.bump(1)] if jc == 1 else [])
                schist[h] = (pe, pe.val)
                if h % 4 == 3:
                    for hh in range(h - 3, h + 1):
                        sch.op(VEC, "scalar_tensor_tensor",
                               S[:, hh * 128:(hh + 1) * 128],
                               S[:, hh * 128:(hh + 1) * 128],
                               dtT[:, 48 + hh:49 + hh],
                               bk.t[:, (hh % 4) * 128:(hh % 4 + 1) * 128],
                               OP.mult, OP.add,
                               waits=[pe.at(), (ac, ac.val)],
                               incs=[ve.bump(1)])
                    bk.release(ve)

        def emit_ln(rb, dst_dram, tok_slice, ab, c_row, otmp, osl, owc,
                    ofree, out_f32):
            bk1, bk2 = pb[4], pb[5]
            w1 = bk1.take() + [ve.at()]
            w2 = bk2.take() + [ac.at()]
            pehist = {}
            for chn in range(16):
                sl = np.s_[:, chn * 512:(chn + 1) * 512]
                sq = sqt[:, (chn % 2) * 512:(chn % 2 + 1) * 512]
                sch.op(SCA, "activation", sq, rb[sl], AF.Square,
                       waits=[ve.at()] + ([pehist[chn - 2]] if chn >= 2 else []),
                       incs=[ac.bump(1)])
                sch.op(TEN, "matmul", bk1.t[0:1, 0:512], onescb[:, 0:1],
                       rb[sl], start=(chn == 0), stop=(chn == 15),
                       waits=(w1 if chn == 0 else []))
                sch.op(TEN, "matmul", bk2.t[0:1, 0:512], onescb[:, 0:1], sq,
                       start=(chn == 0), stop=(chn == 15),
                       waits=(w2 if chn == 0 else []) + [ac.at()],
                       incs=[pe.bump(1)])
                pehist[chn] = (pe, pe.val)
            A = ab[0:1, 0:512]
            B = ab[0:1, 512:1024]
            sch.op(VEC, "tensor_scalar", A, bk1.t[0:1, 0:512], 1.0 / 2048.0,
                   None, OP.mult, waits=[pe.at()], incs=[ve.bump(1)])
            bk1.release(ve)
            sch.op(VEC, "tensor_scalar", B, bk2.t[0:1, 0:512], 1.0 / 2048.0,
                   None, OP.mult, incs=[ve.bump(1)])
            bk2.release(ve)
            sch.op(VEC, "tensor_tensor", c_row, A, A, OP.mult,
                   incs=[ve.bump(1)])
            sch.op(VEC, "tensor_tensor", B, B, c_row, OP.subtract,
                   incs=[ve.bump(1)])
            sch.op(SCA, "activation", B, B, AF.Sqrt, bias=epsb[0:1, 0:1],
                   waits=[ve.at()], incs=[ac.bump(1)])
            sch.op(VEC, "reciprocal", B, B, waits=[ac.at()],
                   incs=[ve.bump(1)])
            sch.op(VEC, "tensor_tensor", A, A, B, OP.mult, incs=[ve.bump(1)])
            sch.op(VEC, "tensor_scalar", A, A, -1.0, None, OP.mult,
                   incs=[ve.bump(1)])
            bkr, bkn = pb[6], pb[7]
            sch.op(TEN, "matmul", bkr.t[:, 0:512], ones1b[0:1, :], B,
                   start=True, stop=True, waits=bkr.take() + [ve.at()],
                   incs=[pe.bump(1)])
            sch.op(TEN, "matmul", bkn.t[:, 0:512], ones1b[0:1, :], A,
                   start=True, stop=True, waits=bkn.take(),
                   incs=[pe.bump(1)])
            for chn in range(16):
                sl = np.s_[:, chn * 512:(chn + 1) * 512]
                oslot = chn % 2
                t1 = otmp[:, 0:512]
                ot = osl[oslot]
                wc = owc[oslot]
                sch.op(VEC, "tensor_tensor", t1, rb[sl], bkr.t[:, 0:512],
                       OP.mult, waits=[pe.at()], incs=[ve.bump(1)])
                sch.op(VEC, "tensor_tensor", ot, t1, bkn.t[:, 0:512], OP.add,
                       waits=[ofree[oslot]], incs=[ve.bump(1)])
                sch.op(SYN, "dma_start",
                       dst_dram[chn * 128:(chn + 1) * 128, tok_slice], ot,
                       waits=[ve.at()], incs=[wc.bump(16)])
                ofree[oslot] = wc.at()
            bkr.release(ve)
            bkn.release(ve)

        esA = ExitStack()
        sbufA = lambda n, sh, dt: esA.enter_context(nc.sbuf_tensor(n, sh, dt))
        xhT = sbufA("xhT", [128, 1024], BF16)
        BT = sbufA("BT", [128, 1024], BF16)
        Bsb = sbufA("Bsb", [128, 512], BF16)
        convtmp = sbufA("convtmp", [128, 1024], F32)
        wblk = sbufA("wblk", [128, 2 * 4096], BF16)
        wdtb = sbufA("wdtb", [128, 256], BF16)
        wfree = [None, None]
        winc = [ctr["c_win0"], ctr["c_win1"]]

        def load_wblk(slot, src_dram, c0, ncol=256):
            sch.op(GPS, "dma_start",
                   wblk[:, slot * 4096:slot * 4096 + 16 * ncol]
                   .rearrange("p (k c) -> p k c", k=16),
                   src_dram[:, c0:c0 + ncol]
                   .rearrange("(k p) c -> p k c", p=128),
                   waits=[wfree[slot]], incs=[winc[slot].bump(16)])

        def wtile(slot, kc, sub):
            base = slot * 4096 + kc * 256 + sub * 128
            return wblk[:, base:base + 128]

        esR = ExitStack()
        xpreb = esR.enter_context(nc.sbuf_tensor("xpreb", [128, 16 * T], BF16))
        hbcpre = esR.enter_context(
            nc.sbuf_tensor("hbcpre", [128, 8 * PREW], BF16))
        wsf = esR.enter_context(nc.sbuf_tensor("wsf", [128, 2 * 2048], F32))
        wsb = esR.enter_context(nc.sbuf_tensor("wsb", [128, 2 * 2048], BF16))
        wupw = ctr["c_x1w1"]
        wsldc = [ctr["c_wup0"], ctr["c_wup1"]]
        ws_state = {"i": 0, "ldfree": [None, None], "sgfree": [None, None]}

        def emit_wup_sign_unit():
            i = ws_state["i"]
            if i >= 64:
                return
            ws_state["i"] = i + 1
            slot = i % 2
            sch.op(GPS, "dma_start",
                   wsf[:, slot * 2048:(slot + 1) * 2048]
                   .rearrange("p (k c) -> p k c", k=16),
                   w_up_t[:, i * 128:(i + 1) * 128]
                   .rearrange("(k p) c -> p k c", p=128),
                   waits=[ws_state["ldfree"][slot]],
                   incs=[wsldc[slot].bump(16)])
            sch.op(SCA, "sign", wsb[:, slot * 2048:(slot + 1) * 2048],
                   wsf[:, slot * 2048:(slot + 1) * 2048],
                   waits=[wsldc[slot].at(), ws_state["sgfree"][slot]],
                   incs=[ac.bump(1)])
            ws_state["ldfree"][slot] = ac.at()
            sch.op(SYN, "dma_start", wup_scr[i * 128:(i + 1) * 128, :],
                   wsb[:, slot * 2048:(slot + 1) * 2048],
                   waits=[(ac, ac.val)], incs=[wupw.bump(16)])
            ws_state["sgfree"][slot] = wupw.at()
        cx = ctr["c_xg"]
        sch.op(GPS, "dma_start", xpreb.rearrange("p (k t) -> p k t", k=16),
               xpre_fm.rearrange("(k p) t -> p k t", p=128),
               incs=[cx.bump(16)])
        for kc in range(16):
            sch.op(VEC, "tensor_copy", xpreh[:, kc * 3:(kc + 1) * 3],
                   xpreb[:, kc * T + 2045:kc * T + 2048],
                   waits=[cx.at()], incs=[ve.bump(1)])

        for b in range(4):
            slot = b % 2
            load_wblk(slot, w_in, 2048 + b * 256)
            for sub in range(2):
                ch = b * 2 + sub
                for tt in range(4):
                    bk = pb[(b * 8 + sub * 4 + tt) % 4]
                    w0 = bk.take() + [winc[slot].at(), cx.at()]
                    for kc in range(16):
                        sch.op(TEN, "matmul", bk.t[:], wtile(slot, kc, sub),
                               xpreb[:, kc * T + tt * 512:
                                     kc * T + (tt + 1) * 512],
                               start=(kc == 0), stop=(kc == 15),
                               waits=(w0 if kc == 0 else []),
                               incs=[pe.bump(1)] if kc == 15 else [])
                    dst = hbcpre[:, ch * PREW + 3 + tt * 512:
                                 ch * PREW + 3 + (tt + 1) * 512]
                    sch.op(VEC, "tensor_copy", dst, bk.t[:], waits=[pe.at()],
                           incs=[ve.bump(1)])
                    bk.release(ve)
            wfree[slot] = pe.at()
            for _ in range(8):
                emit_wup_sign_unit()
        sch.op(GPS, "dma_start", wdtb.rearrange("p (k c) -> p k c", k=16),
               w_in[:, 5120:5136].rearrange("(k p) c -> p k c", p=128),
               waits=[wfree[0]], incs=[winc[0].bump(16)])
        for tt in range(4):
            bk = pb[tt % 4]
            w0 = bk.take() + [winc[0].at()]
            for kc in range(16):
                sch.op(TEN, "matmul", bk.t[0:16, 0:512],
                       wdtb[:, kc * 16:(kc + 1) * 16],
                       xpreb[:, kc * T + tt * 512:kc * T + (tt + 1) * 512],
                       start=(kc == 0), stop=(kc == 15),
                       waits=(w0 if kc == 0 else []),
                       incs=[pe.bump(1)] if kc == 15 else [])
            dt_block_evac(bk, np.s_[tt * 512:(tt + 1) * 512],
                          dte1[:], dte2[:])
        wfree[0] = pe.at()

        for ch in range(8):
            sch.op(VEC, "tensor_copy", halo24[:, ch * 3:ch * 3 + 3],
                   hbcpre[:, ch * PREW + 3 + 2045:ch * PREW + 3 + 2048],
                   waits=[ac.at()], incs=[ve.bump(1)])
        for ch in range(8):
            sch.op(VEC, "memset", hbcpre[:, ch * PREW:ch * PREW + 3], 0.0,
                   incs=[ve.bump(1)])
        for ch in range(8):
            conv_channel(hbcpre, PREW, ch, 1024, 1024, convtmp)
            conv_channel(hbcpre, PREW, ch, 0, 1024, convtmp)
        if DBG:
            sch.op(SYN, "dma_start", dbg_rdt[:, :], dtfull[:],
                   waits=[ac.at()], incs=[ctr["c_ow0"].bump(16)])
            sch.op(SYN, "dma_start", dbg_pre[:, :], hbcpre[:],
                   waits=[ac.at()], incs=[ctr["c_ow0"].bump(16)])
        for c in range(8):
            dt_chunk(c * Q, need_scores=False)
            if DBG and c == 0:
                sch.op(SYN, "dma_start", dbg_dtt[:, :], dtT[:],
                       waits=[ve.at()], incs=[ctr["c_ow0"].bump(16)])
                sch.op(SYN, "dma_start", dbg_dld[:, 0:256], dldtch[:],
                       waits=[ve.at()], incs=[ctr["c_ow0"].bump(16)])
                sch.op(SYN, "dma_start", dbg_dld[:, 256:512], cumch[:],
                       waits=[ve.at()], incs=[ctr["c_ow0"].bump(16)])
                sch.op(SYN, "dma_start", dbg_dld[:, 512:768], ecumch[:],
                       waits=[ve.at()], incs=[ctr["c_ow0"].bump(16)])
                sch.op(SYN, "dma_start", dbg_dld[:, 768:1024], dach[:],
                       waits=[ve.at()], incs=[ctr["c_ow0"].bump(16)])
            trc = ctr["c_tr0"] if c % 2 == 0 else ctr["c_tr1"]
            transposes_chunk(hbcpre, PREW, c, trc)
            sschunk_state(trc)
            for _ in range(4):
                emit_wup_sign_unit()
            if DBG and c == 0:
                sch.op(SYN, "dma_start", dbg_xh[:, :], xhT[:],
                       waits=[trc.at()], incs=[ctr["c_ow0"].bump(16)])
                sch.op(SYN, "dma_start", dbg_bt[:, :], BT[:],
                       waits=[trc.at()], incs=[ctr["c_ow0"].bump(16)])
                sch.op(SYN, "dma_start", dbg_s2[:, :], S[:],
                       waits=[ve.at()], incs=[ctr["c_ow0"].bump(16)])
        if DBG and False:
            sch.op(SYN, "dma_start", dbg_s2[:, :], S[:],
                   waits=[ve.at()], incs=[ctr["c_ow0"].bump(16)])
        sch.op(VEC, "tensor_scalar", S[:], S[:], s0selb[:, 0:1], None,
               OP.mult, waits=[ctr["c_ow0"].at()] if DBG else [],
               incs=[ve.bump(1)])
        if DBG:
            sch.op(SYN, "dma_start", dbg_s[:, :], S[:], waits=[ve.at()],
                   incs=[ctr["c_ow0"].bump(16)])
        while ws_state["i"] < 64:
            emit_wup_sign_unit()
        sch.op(SYN, "wait_ge", wupw.sem, wupw.val)
        esR.close()
        sch.barrier(ctr["bar"])

        esG = ExitStack()
        sbufG = lambda n, sh, dt: esG.enter_context(nc.sbuf_tensor(n, sh, dt))
        S_bf = sbufG("S_bf", [128, NH * 128], BF16)
        DIb = sbufG("DIb", [128, 16 * 128], BF16)
        sdf = sbufG("sdf", [128, 512], F32)
        sxp = sbufG("sxp", [128, 512], BF16)
        mpb = sbufG("mpb", [128, 2 * 512], BF16)
        ctb = sbufG("ctb", [128, 2 * 256], BF16)
        gpre = sbufG("gpre", [128, 16 * 256], BF16)
        g2q = sbufG("g2q", [128, 2 * 256], BF16)
        x1o = sbufG("x1o", [128, 2 * 512], BF16)
        hbc = sbufG("hbc", [128, 24 * HBW], BF16)
        x1wc = [ctr["c_x1w0"], ctr["c_x1w1"]]

        for h in range(16):
            sch.op(VEC, "tensor_scalar", DIb[:, h * 128:(h + 1) * 128],
                   identbf[:], drepb[:, h:h + 1], None, OP.mult,
                   incs=[ve.bump(1)])

        xg_free = None
        x1free = [None, None]
        f2 = {"i": 0, "ldfree": [None, None], "x1free": None,
              "hslotfree": [None, None],
              "bufs": None, "x1tt": -1}
        hwc = [ctr["c_hw0"], ctr["c_hw1"], ctr["c_hw2"], ctr["c_hw3"]]
        f2ld = [ctr["c_wup0"], ctr["c_wup1"]]
        f2x1 = ctr["c_ldr"]

        def emit_f2_unit(nmax=64 * 4):
            i = f2["i"]
            if i >= nmax:
                return False
            f2["i"] = i + 1
            tt, hcol = i // 64, i % 64
            x1f_t, wupr, hidt_t = f2["bufs"]
            if f2["x1tt"] != tt:
                f2["x1tt"] = tt
                sch.op(SYN, "dma_start",
                       x1f_t.rearrange("p (k t) -> p k t", k=16),
                       x1_scr[:, tt * 512:(tt + 1) * 512]
                       .rearrange("(k p) t -> p k t", p=128),
                       waits=[x1wc[0].at(), x1wc[1].at(), f2["x1free"]],
                       incs=[f2x1.bump(16)])
            slot = i % 2
            sch.op(SYN, "dma_start",
                   wupr[:, slot * 2048:(slot + 1) * 2048],
                   wup_scr[hcol * 128:(hcol + 1) * 128, :],
                   waits=[(wupw, 64 * 16), f2["ldfree"][slot]],
                   incs=[f2ld[slot].bump(16)])
            bk = pb[1]
            w0 = bk.take() + [f2ld[slot].at(), f2x1.at()]
            for kc in range(16):
                sch.op(TEN, "matmul", bk.t[:],
                       wupr[:, slot * 2048 + kc * 128:
                            slot * 2048 + (kc + 1) * 128],
                       x1f_t[:, kc * 512:(kc + 1) * 512],
                       start=(kc == 0), stop=(kc == 15),
                       waits=(w0 if kc == 0 else []),
                       incs=[pe.bump(1)] if kc == 15 else [])
            f2["ldfree"][slot] = (pe, pe.val)
            f2["x1free"] = (pe, pe.val)
            hs = i % 2
            ht = hidt_t[:, hs * 512:(hs + 1) * 512]
            sch.op(SCA, "activation", ht, bk.t[:], AF.Relu,
                   scale=upscb[:, hcol:hcol + 1],
                   waits=[pe.at(), f2["hslotfree"][hs], (ac, ac.val)],
                   incs=[ac.bump(1)])
            bk.release(ac)
            sch.op(GPS, "dma_start",
                   hid_scr[hcol * 128:(hcol + 1) * 128,
                           tt * 512:(tt + 1) * 512],
                   ht, waits=[ac.at()], incs=[hwc[tt].bump(16)])
            f2["hslotfree"][hs] = hwc[tt].at()
            return True

        for g in range(2):
            esGg = ExitStack()
            gateb = esGg.enter_context(
                nc.sbuf_tensor(f"gate{g}", [128, 16 * TG], BF16))
            esXW = ExitStack()
            xgb = esXW.enter_context(
                nc.sbuf_tensor(f"xg{g}", [128, 16 * TG], BF16))
            sch.op(GPS, "dma_start", xgb.rearrange("p (k t) -> p k t", k=16),
                   x_fm[:, g * TG:(g + 1) * TG]
                   .rearrange("(k p) t -> p k t", p=128),
                   waits=[xg_free], incs=[cx.bump(16)])

            col_blocks = ([("hbc", 2048 + b * 256, b) for b in range(12)]
                          + [("dt", 5120, 0)]
                          + [("gate", b * 256, b) for b in range(8)])
            bkrot = 0
            for kind, c0, bidx in col_blocks:
                if kind == "dt":
                    sch.op(GPS, "dma_start",
                           wdtb.rearrange("p (k c) -> p k c", k=16),
                           w_in[:, 5120:5136]
                           .rearrange("(k p) c -> p k c", p=128),
                           waits=[wfree[0]], incs=[winc[0].bump(16)])
                    for tt in range(2):
                        bk = pb[bkrot % 4]; bkrot += 1
                        w0 = bk.take() + [winc[0].at(), cx.at()]
                        for kc in range(16):
                            sch.op(TEN, "matmul", bk.t[0:16, 0:512],
                                   wdtb[:, kc * 16:(kc + 1) * 16],
                                   xgb[:, kc * TG + tt * 512:
                                       kc * TG + (tt + 1) * 512],
                                   start=(kc == 0), stop=(kc == 15),
                                   waits=(w0 if kc == 0 else []),
                                   incs=[pe.bump(1)] if kc == 15 else [])
                        dt_block_evac(bk, np.s_[tt * 512:(tt + 1) * 512],
                                  dte1[:], dte2[:])
                    wfree[0] = pe.at()
                    continue
                slot = bidx % 2
                load_wblk(slot, w_in if kind == "hbc" else w_in, c0)
                for sub in range(2):
                    for tt in range(2):
                        bk = pb[bkrot % 4]; bkrot += 1
                        w0 = bk.take() + [winc[slot].at(), cx.at()]
                        for kc in range(16):
                            sch.op(TEN, "matmul", bk.t[:],
                                   wtile(slot, kc, sub),
                                   xgb[:, kc * TG + tt * 512:
                                       kc * TG + (tt + 1) * 512],
                                   start=(kc == 0), stop=(kc == 15),
                                   waits=(w0 if kc == 0 else []),
                                   incs=[pe.bump(1)] if kc == 15 else [])
                        if kind == "hbc":
                            ch = bidx * 2 + sub
                            dst = hbc[:, ch * HBW + 3 + tt * 512:
                                      ch * HBW + 3 + (tt + 1) * 512]
                            sch.op(VEC, "tensor_copy", dst, bk.t[:],
                                   waits=[pe.at()], incs=[ve.bump(1)])
                            bk.release(ve)
                        else:
                            ch = bidx * 2 + sub
                            dst = gateb[:, ch * TG + tt * 512:
                                        ch * TG + (tt + 1) * 512]
                            sch.op(SCA, "activation", dst, bk.t[:], AF.Silu,
                                   waits=[pe.at()], incs=[ac.bump(1)])
                            bk.release(ac)
                    if g == 0 and kind == "hbc" and bidx >= 4:
                        ch = bidx * 2 + sub
                        bk = pb[4]
                        w0 = bk.take()
                        for kc in range(16):
                            sch.op(TEN, "matmul", bk.t[:, 0:3],
                                   wtile(slot, kc, sub),
                                   xpreh[:, kc * 3:(kc + 1) * 3],
                                   start=(kc == 0), stop=(kc == 15),
                                   waits=(w0 if kc == 0 else []),
                                   incs=[pe.bump(1)] if kc == 15 else [])
                        sch.op(VEC, "tensor_copy",
                               hbc[:, ch * HBW:ch * HBW + 3], bk.t[:, 0:3],
                               waits=[pe.at()], incs=[ve.bump(1)])
                        bk.release(ve)
                if kind == "hbc":
                    for sub in range(2):
                        ch = bidx * 2 + sub
                        if g == 1 or ch < 8:
                            sch.op(VEC, "tensor_copy",
                                   hbc[:, ch * HBW:ch * HBW + 3],
                                   halo24[:, ch * 3:ch * 3 + 3],
                                   incs=[ve.bump(1)])
                        if g == 0:
                            sch.op(VEC, "tensor_copy",
                                   halo24[:, ch * 3:ch * 3 + 3],
                                   hbc[:, ch * HBW + 3 + TG - 3:
                                       ch * HBW + 3 + TG],
                                   incs=[ve.bump(1)])
                        conv_channel(hbc, HBW, ch, 0, TG, convtmp)
                wfree[slot] = pe.at()
            esXW.close()
            xg_free = pe.at()
            esF2i = ExitStack()
            if g == 1:
                x1f_i = esF2i.enter_context(
                    nc.sbuf_tensor("x1fi", [128, 16 * 512], BF16))
                wupr_i = esF2i.enter_context(
                    nc.sbuf_tensor("wupri", [128, 2 * 2048], BF16))
                hidt_i = esF2i.enter_context(
                    nc.sbuf_tensor("hidti", [128, 2 * 512], BF16))
                f2["bufs"] = (x1f_i, wupr_i, hidt_i)
            if DBG and g == 0:
                sch.op(SYN, "dma_start", dbg_hbc[:, :], hbc[:],
                       waits=[ac.at()], incs=[ctr["c_ow0"].bump(16)])
                sch.op(SYN, "dma_start", dbg_dt[:, 0:TG], dtfull[:, 0:TG],
                       waits=[ac.at()], incs=[ctr["c_ow0"].bump(16)])

            for c in range(4):
                sch.op(VEC, "tensor_copy", S_bf[:], S[:],
                       waits=[(pe, pe.val)], incs=[ve.bump(1)])
                dt_chunk(c * Q, need_scores=True)
                trc = ctr["c_tr0"] if c % 2 == 0 else ctr["c_tr1"]
                transposes_chunk(hbc, HBW, c, trc)
                yhist = {}
                sc2hist = {}
                for h in range(16):
                    g4 = h // 4
                    slot = h % 2
                    sell = selb[:, h * 128:(h + 1) * 128]
                    bkbe = pb[4 + slot]
                    w0 = bkbe.take() + [ve.at()]
                    for i in range(3):
                        sch.op(TEN, "matmul", bkbe.t[:, 0:256], sell,
                               cumsp[:, i * Q:(i + 1) * Q],
                               start=(i == 0), stop=(i == 2),
                               waits=(w0 if i == 0 else []))
                    for i in range(2):
                        sch.op(TEN, "matmul", bkbe.t[:, 256:512], sell,
                               ecusp[:, i * Q:(i + 1) * Q],
                               start=(i == 0), stop=(i == 1),
                               incs=[pe.bump(1)] if i == 1 else [])
                    bks = pb[0]
                    w0 = bks.take()
                    for jc in range(2):
                        sch.op(TEN, "matmul",
                               bks.t[:, jc * 256:(jc + 1) * 256],
                               hbc[:, (4 + g4) * HBW + 3 + c * Q + jc * 128:
                                   (4 + g4) * HBW + 3 + c * Q + (jc + 1) * 128],
                               hbc[:, (8 + h) * HBW + 3 + c * Q:
                                   (8 + h) * HBW + 3 + (c + 1) * Q],
                               start=True, stop=True,
                               waits=(w0 if jc == 0 else []),
                               incs=[pe.bump(1)] if jc == 1 else [])
                    for jc in range(2):
                        sch.op(VEC, "scalar_tensor_tensor",
                               sdf[:, jc * 256:(jc + 1) * 256],
                               bkbe.t[:, 0:256],
                               dtT[:, jc * 64 + h:jc * 64 + h + 1],
                               maskb[:, jc * 256:(jc + 1) * 256],
                               OP.subtract, OP.add,
                               waits=[pe.at(), (ac, ac.val)],
                               incs=[ve.bump(1)])
                    sch.op(SCA, "activation", sxp[:], sdf[:], AF.Exp,
                           waits=[ve.at()], incs=[ac.bump(1)])
                    for jc in range(2):
                        sch.op(VEC, "scalar_tensor_tensor",
                               mpb[:, slot * 512 + jc * 256:
                                   slot * 512 + (jc + 1) * 256],
                               bks.t[:, jc * 256:(jc + 1) * 256],
                               dtT[:, jc * 64 + 32 + h:jc * 64 + 33 + h],
                               sxp[:, jc * 256:(jc + 1) * 256],
                               OP.mult, OP.mult,
                               waits=[ac.at()] + ([yhist[h - 2]]
                                                  if h >= 2 else []),
                               incs=[ve.bump(1)])
                    bks.release(ve)
                    sch.op(VEC, "tensor_tensor",
                           ctb[:, slot * 256:(slot + 1) * 256],
                           hbc[:, (8 + h) * HBW + 3 + c * Q:
                               (8 + h) * HBW + 3 + (c + 1) * Q],
                           bkbe.t[:, 256:512], OP.mult, incs=[ve.bump(1)])
                    bkbe.release(ve)
                    bky = pb[2 + slot]
                    w0 = bky.take()
                    for jc in range(2):
                        sch.op(TEN, "matmul", bky.t[:, 0:256],
                               xhT[:, (g4 * 2 + jc) * 128:
                                   (g4 * 2 + jc + 1) * 128],
                               mpb[:, slot * 512 + jc * 256:
                                   slot * 512 + (jc + 1) * 256],
                               start=(jc == 0), stop=False,
                               waits=(w0 if jc == 0 else []) + [ve.at(),
                                                                trc.at()])
                    sch.op(TEN, "matmul", bky.t[:, 0:256],
                           DIb[:, h * 128:(h + 1) * 128],
                           hbc[:, g4 * HBW + 3 + c * Q:
                               g4 * HBW + 3 + (c + 1) * Q],
                           start=False, stop=False)
                    sch.op(TEN, "matmul", bky.t[:, 0:256],
                           S_bf[:, h * 128:(h + 1) * 128],
                           ctb[:, slot * 256:(slot + 1) * 256],
                           start=False, stop=True, incs=[pe.bump(1)])
                    yhist[h] = (pe, pe.val)
                    sch.op(VEC, "tensor_tensor",
                           gpre[:, h * 256:(h + 1) * 256], bky.t[:, 0:256],
                           gateb[:, h * TG + c * Q:h * TG + (c + 1) * Q],
                           OP.mult, waits=[pe.at(), (ac, ac.val)],
                           incs=[ve.bump(1)])
                    bky.release(ve)
                    for jc in range(2):
                        sch.op(VEC, "tensor_scalar",
                               Bsb[:, slot * 256 + jc * 128:
                                   slot * 256 + (jc + 1) * 128],
                               BT[:, (g4 * 2 + jc) * 128:
                                   (g4 * 2 + jc + 1) * 128],
                               dtT[:, jc * 64 + 16 + h:jc * 64 + 17 + h],
                               None, OP.mult,
                               waits=[(ac, ac.val)]
                               + ([sc2hist[h - 2]] if h >= 2 else []),
                               incs=[ve.bump(1)])
                    bksc = pb[6 + (h // 4) % 2]
                    qoff = (h % 4) * 128
                    w0 = (bksc.take() if h % 4 == 0 else [])
                    for jc in range(2):
                        sch.op(TEN, "matmul", bksc.t[:, qoff:qoff + 128],
                               Bsb[:, slot * 256 + jc * 128:
                                   slot * 256 + (jc + 1) * 128],
                               xhT[:, (g4 * 2 + jc) * 128:
                                   (g4 * 2 + jc + 1) * 128],
                               start=(jc == 0), stop=(jc == 1),
                               waits=(w0 if jc == 0 else []) + [ve.at()],
                               incs=[pe.bump(1)] if jc == 1 else [])
                    sc2hist[h] = (pe, pe.val)
                    if h % 4 == 3:
                        for hh in range(h - 3, h + 1):
                            sch.op(VEC, "scalar_tensor_tensor",
                                   S[:, hh * 128:(hh + 1) * 128],
                                   S[:, hh * 128:(hh + 1) * 128],
                                   dtT[:, 48 + hh:49 + hh],
                                   bksc.t[:, (hh % 4) * 128:
                                          (hh % 4 + 1) * 128],
                                   OP.mult, OP.add,
                                   waits=[pe.at(), (ac, ac.val)],
                                   incs=[ve.bump(1)])
                        bksc.release(ve)
                    if g == 1 and f2["bufs"] is not None:
                        for _ in range(8):
                            emit_f2_unit(nmax=128)
                bkgn = pb[4]
                wgn = bkgn.take()
                gn_hist = {}
                for h in range(16):
                    sq = g2q[:, (h % 2) * 256:(h % 2 + 1) * 256]
                    sch.op(SCA, "activation", sq,
                           gpre[:, h * 256:(h + 1) * 256], AF.Square,
                           waits=[ve.at()] + ([gn_hist[h - 2]]
                                              if h >= 2 else []),
                           incs=[ac.bump(1)])
                    sch.op(TEN, "matmul", bkgn.t[0:16, 0:256],
                           ohb[:, h * 16:(h + 1) * 16], sq,
                           start=(h == 0), stop=(h == 15),
                           waits=(wgn if h == 0 else []) + [ac.at()],
                           incs=[pe.bump(1)])
                    gn_hist[h] = (pe, pe.val)
                sch.op(SCA, "activation", gnrs[:], bkgn.t[0:16, 0:256],
                       AF.Sqrt, bias=epsb[0:16, 0:1],
                       scale=scl128b[0:16, 0:1],
                       waits=[pe.at()], incs=[ac.bump(1)])
                bkgn.release(ac)
                sch.op(VEC, "reciprocal", gnrs[:], gnrs[:], waits=[ac.at()],
                       incs=[ve.bump(1)])
                split2(rssp, gnrs[:])
                for h in range(16):
                    bkr = pb[5 + (h % 2) * 2]
                    sell = selb[:, h * 128:(h + 1) * 128]
                    w0 = bkr.take() + [ve.at()]
                    for i in range(2):
                        sch.op(TEN, "matmul", bkr.t[:, 0:256], sell,
                               rssp[:, i * Q:(i + 1) * Q],
                               start=(i == 0), stop=(i == 1),
                               waits=(w0 if i == 0 else []),
                               incs=[pe.bump(1)] if i == 1 else [])
                    sch.op(VEC, "scalar_tensor_tensor",
                           hbc[:, (8 + h) * HBW + 3 + c * Q:
                               (8 + h) * HBW + 3 + (c + 1) * Q],
                           bkr.t[:, 0:256], nwstb[:, h:h + 1],
                           gpre[:, h * 256:(h + 1) * 256],
                           OP.mult, OP.mult, waits=[pe.at()],
                           incs=[ve.bump(1)])
                    bkr.release(ve)
                if DBG and g == 0 and c == 0:
                    sch.op(SYN, "dma_start", dbg_g[:, :], gpre[:],
                           waits=[ve.at()], incs=[ctr["c_ow0"].bump(16)])
            f2["bufs"] = None
            esF2i.close()
            esGg.close()

            esO = ExitStack()
            r1b = esO.enter_context(
                nc.sbuf_tensor(f"r1_{g}", [128, 16 * 512], BF16))
            xresb = esO.enter_context(
                nc.sbuf_tensor(f"xres_{g}", [128, 16 * 512], BF16))
            for tt in range(2):
                sch.op(GPS, "dma_start",
                       xresb.rearrange("p (k t) -> p k t", k=16),
                       x_fm[:, g * TG + tt * 512:g * TG + (tt + 1) * 512]
                       .rearrange("(k p) t -> p k t", p=128),
                       waits=[(pe, pe.val), (ve, ve.val)],
                       incs=[cx.bump(16)])
                for dblk in range(8):
                    slot = dblk % 2
                    load_wblk(slot, w_out, dblk * 256)
                    for sub in range(2):
                        dcol = dblk * 2 + sub
                        bk = pb[bkrot % 4]; bkrot += 1
                        w0 = bk.take() + [winc[slot].at(), ve.at(), cx.at()]
                        for kc in range(16):
                            sch.op(TEN, "matmul", bk.t[:],
                                   wtile(slot, kc, sub),
                                   hbc[:, (8 + kc) * HBW + 3 + tt * 512:
                                       (8 + kc) * HBW + 3 + (tt + 1) * 512],
                                   start=(kc == 0), stop=(kc == 15),
                                   waits=(w0 if kc == 0 else []),
                                   incs=[pe.bump(1)] if kc == 15 else [])
                        sch.op(VEC, "scalar_tensor_tensor",
                               r1b[:, dcol * 512:(dcol + 1) * 512],
                               bk.t[:], 1.0,
                               xresb[:, dcol * 512:(dcol + 1) * 512],
                               OP.mult, OP.add, waits=[pe.at(), cx.at()],
                               incs=[ve.bump(1)])
                        bk.release(ve)
                    wfree[slot] = pe.at()
                emit_ln(r1b, x1_scr,
                        np.s_[g * TG + tt * 512:g * TG + (tt + 1) * 512],
                        convtmp, crowb[:], convtmp,
                        [x1o[:, 0:512], x1o[:, 512:1024]], x1wc, x1free,
                        out_f32=False)
            esO.close()
        esG.close()
        esA.close()
        sch.op(SYN, "wait_ge", x1wc[0].sem, x1wc[0].val)
        sch.op(SYN, "wait_ge", x1wc[1].sem, x1wc[1].val)
        sch.barrier(ctr["bar"])

        esF2 = ExitStack()
        x1f_2 = esF2.enter_context(
            nc.sbuf_tensor("x1f2", [128, 16 * 512], BF16))
        wupr_2 = esF2.enter_context(
            nc.sbuf_tensor("wupr2", [128, 2 * 2048], BF16))
        hidt_2 = esF2.enter_context(
            nc.sbuf_tensor("hidt2", [128, 2 * 512], BF16))
        f2["bufs"] = (x1f_2, wupr_2, hidt_2)
        f2["x1tt"] = -1
        f2["hslotfree"] = [None, None]
        f2["ldfree"] = [None, None]
        f2["x1free"] = None
        while emit_f2_unit():
            pass
        esF2.close()
        for i in range(4):
            sch.op(GPS, "wait_ge", hwc[i].sem, hwc[i].val)
        sch.barrier(ctr["bar"])

        esF3 = ExitStack()
        hidf = esF3.enter_context(nc.sbuf_tensor("hidf", [128, 64 * 512], BF16))
        x1k = esF3.enter_context(nc.sbuf_tensor("x1k", [128, 16 * 512], BF16))
        wdnf = esF3.enter_context(nc.sbuf_tensor("wdnf", [128, 2 * 2048], F32))
        wdnb = esF3.enter_context(nc.sbuf_tensor("wdnb", [128, 2 * 8192], BF16))
        r2b = esF3.enter_context(nc.sbuf_tensor("r2b", [128, 16 * 512], BF16))
        outt = esF3.enter_context(nc.sbuf_tensor("outt", [128, 2 * 512], F32))
        f3tmp = esF3.enter_context(nc.sbuf_tensor("f3tmp", [128, 1024], F32))
        wdnc = [ctr["c_wdn0"], ctr["c_wdn1"]]
        cl2 = ctr["c_ldr2"]
        wdnw = ctr["c_x1w0"]
        dnfree = [None, None]
        owc = [ctr["c_ow0"], ctr["c_ow1"]]
        outfree = [None, None]
        dn_hist = {}
        wdnw_hist = {}
        def f3_loads(tt):
            sch.op(GPS, "dma_start", hidf.rearrange("p (k t) -> p k t", k=64),
                   hid_scr[:, tt * 512:(tt + 1) * 512]
                   .rearrange("(k p) t -> p k t", p=128),
                   waits=[hwc[0].at(), hwc[1].at(), hwc[2].at(), hwc[3].at(),
                          (pe, pe.val)],
                   incs=[cl2.bump(16)])
            sch.op(GPS, "dma_start", x1k.rearrange("p (k t) -> p k t", k=16),
                   x1_scr[:, tt * 512:(tt + 1) * 512]
                   .rearrange("(k p) t -> p k t", p=128),
                   waits=[(ve, ve.val)], incs=[cl2.bump(16)])

        for tt in range(4):
            if tt == 0:
                f3_loads(0)
            for dcol in range(16):
                wslot = dcol % 2
                if tt == 0:
                    for half in range(4):
                        hs = half % 2
                        sch.op(SYN, "dma_start",
                               wdnf[:, hs * 2048:(hs + 1) * 2048]
                               .rearrange("p (k c) -> p k c", k=16),
                               w_dn_t[half * 2048:(half + 1) * 2048,
                                      dcol * 128:(dcol + 1) * 128]
                               .rearrange("(k p) c -> p k c", p=128),
                               waits=[dnfree[hs]], incs=[wdnc[hs].bump(16)])
                        sch.op(SCA, "sign",
                               wdnb[:, wslot * 8192 + half * 2048:
                                    wslot * 8192 + (half + 1) * 2048],
                               wdnf[:, hs * 2048:(hs + 1) * 2048],
                               waits=[wdnc[hs].at(),
                                      dn_hist.get((tt * 16 + dcol) - 2),
                                      wdnw_hist.get(dcol - 2)],
                               incs=[ac.bump(1)])
                        dnfree[hs] = ac.at()
                    sch.op(SYN, "dma_start",
                           wdn_scr[dcol * 128:(dcol + 1) * 128, :],
                           wdnb[:, wslot * 8192:(wslot + 1) * 8192],
                           waits=[(ac, ac.val)], incs=[wdnw.bump(16)])
                    wdnw_hist[dcol] = wdnw.at()
                else:
                    sch.op(SYN, "dma_start",
                           wdnb[:, wslot * 8192:(wslot + 1) * 8192],
                           wdn_scr[dcol * 128:(dcol + 1) * 128, :],
                           waits=[wdnw.at(),
                                  dn_hist.get((tt * 16 + dcol) - 2)],
                           incs=[wdnc[0].bump(16)])
                bk = pb[dcol % 4]
                w0 = bk.take() + ([ac.at()] if tt == 0
                                  else [wdnc[0].at()]) + [cl2.at()]
                for hk in range(64):
                    sch.op(TEN, "matmul", bk.t[:],
                           wdnb[:, wslot * 8192 + hk * 128:
                                wslot * 8192 + (hk + 1) * 128],
                           hidf[:, hk * 512:(hk + 1) * 512],
                           start=(hk == 0), stop=(hk == 63),
                           waits=(w0 if hk == 0 else []),
                           incs=[pe.bump(1)] if hk == 63 else [])
                dn_hist[tt * 16 + dcol] = (pe, pe.val)
                sch.op(VEC, "scalar_tensor_tensor",
                       r2b[:, dcol * 512:(dcol + 1) * 512], bk.t[:],
                       dsob[:, dcol:dcol + 1],
                       x1k[:, dcol * 512:(dcol + 1) * 512],
                       OP.mult, OP.add, waits=[pe.at(), cl2.at()],
                       incs=[ve.bump(1)])
                bk.release(ve)
            if tt < 3:
                f3_loads(tt + 1)
            emit_ln(r2b, out_fm, np.s_[tt * 512:(tt + 1) * 512],
                    f3tmp, crowb[:], f3tmp,
                    [outt[:, 0:512], outt[:, 512:1024]], owc, outfree,
                    out_f32=True)
        esF3.close()
        sch.op(SYN, "wait_ge", owc[0].sem, owc[0].val)
        sch.op(SYN, "wait_ge", owc[1].sem, owc[1].val)

        with nc.Block() as block:
            @block.tensor
            def _(e):
                sch.replay(TEN, e)

            @block.vector
            def _(e):
                sch.replay(VEC, e)

            @block.scalar
            def _(e):
                sch.replay(SCA, e)

            @block.gpsimd
            def _(e):
                sch.replay(GPS, e)

            @block.sync
            def _(e):
                sch.replay(SYN, e)

    return nc


_NC_CACHE = {}


def kernel(L0_attn_hidden_states, L0_attn_W_in, L0_attn_conv_weight,
           L0_attn_conv_bias, L0_attn_A_log, L0_attn_D, L0_attn_dt_bias,
           L0_attn_norm_weight, L0_attn_W_out, L0_ffn_up_weight,
           L0_ffn_down_weight, L0_ffn_up_scale, L0_ffn_down_scale,
           L0_ffn_output_scale):
    hs = np.asarray(L0_attn_hidden_states, np.float32)
    B, L, _ = hs.shape
    f32c = lambda a: np.ascontiguousarray(np.asarray(a, np.float32))

    shared = {
        "w_in": f32c(L0_attn_W_in),
        "w_out": f32c(L0_attn_W_out),
        "w_up_t": f32c(np.asarray(L0_ffn_up_weight).T),
        "w_dn_t": f32c(np.asarray(L0_ffn_down_weight).T),
        "conv_w": f32c(np.asarray(L0_attn_conv_weight)
                       .reshape(24, 128, 4).transpose(1, 0, 2)
                       .reshape(128, 96)),
        "conv_b": f32c(np.asarray(L0_attn_conv_bias).reshape(24, 128).T),
        "alog_d": f32c(np.asarray(L0_attn_A_log).reshape(NH, 1)),
        "dtb_d": f32c(np.asarray(L0_attn_dt_bias).reshape(NH, 1)),
        "drep_d": f32c(np.broadcast_to(np.asarray(L0_attn_D).reshape(1, NH),
                                       (128, NH))),
        "nwst_d": f32c(np.asarray(L0_attn_norm_weight).reshape(16, 128).T),
        "upsc_d": f32c(np.asarray(L0_ffn_up_scale).reshape(64, 128).T),
        "dnsc_d": f32c(np.asarray(L0_ffn_down_scale).reshape(16, 128).T),
        "outsc_d": f32c(np.broadcast_to(
            np.asarray(L0_ffn_output_scale).reshape(1, 1), (128, 1))),
    }

    in_maps = []
    for core in range(8):
        b, half = core // 2, core % 2
        m = dict(shared)
        m["x_fm"] = f32c(hs[b, half * T:(half + 1) * T, :].T)
        m["xpre_fm"] = (f32c(hs[b, 0:T, :].T) if half == 1
                        else np.zeros((HID, T), np.float32))
        m["s0sel_d"] = np.full((128, 1), float(half), np.float32)
        in_maps.append(m)

    if "v1" not in _NC_CACHE:
        _NC_CACHE["v1"] = build_nc()
    nc = _NC_CACHE["v1"]

    res = run_bass_kernel_spmd(nc, in_maps, core_ids=list(range(8)))

    out = np.empty((B, L, HID), np.float32)
    for core in range(8):
        b, half = core // 2, core % 2
        out[b, half * T:(half + 1) * T, :] = res.results[core]["out_fm"].T
    if DBG:
        kernel._dbg = res.results
    return out
